# version 15
# baseline (speedup 1.0000x reference)
import numpy as np
import ml_dtypes

import concourse.bass as bass
import concourse.mybir as mybir
from concourse.bass_utils import run_bass_kernel_spmd

F32 = mybir.dt.float32
BF16 = mybir.dt.bfloat16
AF = mybir.ActivationFunctionType
ALU = mybir.AluOpType
AX = mybir.AxisListType



class Sem:
    __slots__ = ("name", "n", "h")
    def __init__(self, name):
        self.name, self.n, self.h = name, 0, None


class Rec:
    __slots__ = ("eng", "kind", "args", "waits", "incs")
    def __init__(self, eng, kind, args, waits, incs):
        self.eng, self.kind, self.args, self.waits, self.incs = \
            eng, kind, args, waits, incs


class Sched:
    def __init__(self):
        self.recs = []
        self.sems = []

    def sem(self, name):
        s = Sem(name)
        self.sems.append(s)
        return s

    def op(self, eng, kind, waits=(), inc=None, inc_k=1, **args):
        for sem, v in waits:
            assert v <= sem.n, f"wait {sem.name}>={v} not yet scheduled ({sem.n})"
        incs = []
        if inc is not None:
            inc.n += inc_k
            incs.append((inc, inc_k))
        self.recs.append(Rec(eng, kind, args, [(s, v) for s, v in waits if v > 0], incs))
        return self.recs[-1]

    def dma(self, queue, out, in_, waits=(), inc=None):
        return self.op(queue, "dma", waits=waits, inc=inc, inc_k=16, out=out, in_=in_)



def build_program(S=8192, T=1024, NPASS=4, debug=False):
    E, CH, NEXP, H = 1024, 256, 8, 4096
    NEIN = E // 128
    NT = T // 128
    NT4 = T // 512
    NMP = (S // CH) // 2
    NOWN = T // CH
    NHG = 4
    HCG = 8
    NGRP = T // 512
    CAP = 320
    CC = ((0, 128), (128, 128), (256, 64))
    QSC = float(1.0 / np.sqrt(E))
    NCROSS = S // CH
    assert T % 512 == 0 and S % 512 == 0

    nc = bass.Bass("TRN2", target_bir_lowering=False, debug=False)

    xk = nc.dram_tensor("xk", [S, E], F32, kind="ExternalInput").ap()
    Wq_d = nc.dram_tensor("Wq", [E, E], F32, kind="ExternalInput").ap()
    bq_d = nc.dram_tensor("bq", [E], F32, kind="ExternalInput").ap()
    Wk_d = nc.dram_tensor("Wk", [E, E], F32, kind="ExternalInput").ap()
    bk_d = nc.dram_tensor("bk", [E], F32, kind="ExternalInput").ap()
    Wv_d = nc.dram_tensor("Wv", [E, E], F32, kind="ExternalInput").ap()
    bv_d = nc.dram_tensor("bv", [E], F32, kind="ExternalInput").ap()
    Wo_d = nc.dram_tensor("Wo", [E, E], F32, kind="ExternalInput").ap()
    bo_d = nc.dram_tensor("bo", [E], F32, kind="ExternalInput").ap()
    gw_d = nc.dram_tensor("gw", [NEXP, E], F32, kind="ExternalInput").ap()
    gb_d = nc.dram_tensor("gb", [NEXP], F32, kind="ExternalInput").ap()
    w1_d = nc.dram_tensor("w1", [NEXP, E, H], F32, kind="ExternalInput").ap()
    b1_d = nc.dram_tensor("b1", [NEXP, H], F32, kind="ExternalInput").ap()
    w2_d = nc.dram_tensor("w2", [NEXP, H, E], F32, kind="ExternalInput").ap()
    b2_d = nc.dram_tensor("b2", [NEXP, E], F32, kind="ExternalInput").ap()
    out_d = nc.dram_tensor("out", [NPASS * T, E], F32, kind="ExternalOutput").ap()
    kspill_d = nc.dram_tensor("kspill", [NMP, 128, NEIN, 512], BF16,
                              kind="Internal").ap()

    ident_bf_d = nc.inline_tensor(np.eye(128, dtype=ml_dtypes.bfloat16), "ident_bf")
    ident_f32_d = nc.inline_tensor(np.eye(128, dtype=np.float32), "ident_f32")
    lt_d = nc.inline_tensor(
        np.triu(np.ones((128, 128), np.float32)).astype(ml_dtypes.bfloat16), "lt_inc")
    onesm_d = nc.inline_tensor(np.ones((128, 128), dtype=ml_dtypes.bfloat16), "ones_m")
    iota1_d = nc.inline_tensor(
        np.broadcast_to(np.arange(1, CAP + 1, dtype=np.float32), (128, CAP)).copy(),
        "iota1")

    def at(name, off, shape, dtype):
        return nc.alloc_sbuf_tensor_at(name, shape, dtype, offset=off)

    SBBASE = 16384 + 2048
    SBTOP = 229216
    cur = [SBBASE]
    def nxt(name, shape, dtype, align=64):
        elt = 2 if dtype == BF16 else 4
        n = 1
        for d in shape[1:]:
            n *= d
        cur[0] = (cur[0] + align - 1) // align * align
        t = at(name, cur[0], shape, dtype)
        cur[0] += n * elt
        return t

    xn_fm = nxt("xn_fm", [128, NEIN, T], BF16)
    gate = nxt("gate", [128, NT, NEXP], F32)
    sbq_c = nxt("sbq_c", [128, 8], F32)
    bk_c = nxt("bk_c", [128, 8], F32)
    bv33_c = nxt("bv33_c", [128, 8], F32)
    b1_c = nxt("b1_c", [128, 256], F32)
    b2_all = nxt("b2_all", [8, E], BF16)
    bo_row = nxt("bo_row", [1, E], BF16)
    gb_row = nxt("gb_row", [1, NEXP], BF16)
    gwT = nxt("gwT", [128, NEIN * NEXP], BF16)
    ones_r = nxt("ones_r", [1, 256], BF16)
    bv33_r = nxt("bv33_r", [1, E], BF16)
    glog = nxt("glog", [128, NT, NEXP], F32)
    gexp = nxt("gexp", [128, NT, NEXP], F32)
    gsum = nxt("gsum", [128, NT], F32)
    grec = nxt("grec", [128, NT], F32)
    id_bf = nxt("id_bf", [128, 128], BF16)
    id_f32 = nxt("id_f32", [128, 128], F32)
    lt_sb = nxt("lt_sb", [128, 128], BF16)
    onesm = nxt("onesm", [128, 128], BF16)
    iota1 = nxt("iota1", [128, CAP], F32)
    gateT = nxt("gateT", [8, NT, 128], BF16)
    cum_sb = nxt("cum_sb", [128, NGRP * 4 * NEXP], F32)
    mask_bf = nxt("mask_bf", [128, NT, NEXP], BF16)
    mask_f = nxt("mask_f", [128, NT, NEXP], F32)
    stat = nxt("stat", [128, 64], F32)
    smx = nxt("smx", [128, 2, 8], F32)
    gstat = nxt("gstat", [128, 8], F32)
    gtmp = nxt("gtmp", [128, 8], F32)
    gmask = nxt("gmask", [128, 8], F32)
    ones_r_f32_guard = nxt("guard_src", [128, 1], F32)
    PERS_END = (cur[0] + 63) // 64 * 64

    A0 = PERS_END
    out_acc = at("out_acc", A0, [128, NT, E], F32)
    g_sb = at("g_sb", A0, [128, NT, E], BF16)
    P_acc = at("P_acc", A0 + NT * E * 2, [128, NOWN, 2, CH], F32)
    out_stg = at("out_stg", A0 + NT * E * 2 + NOWN * 2 * CH * 4, [128, 2, E], F32)
    assert NT * E * 2 + NOWN * 2 * CH * 4 + 2 * E * 4 <= NT * E * 4

    U = A0 + NT * E * 4
    xn_tok = at("xn_tok", U, [128, NT, E], BF16)
    Xg = at("Xg", U + 16384, [128, NEIN, NGRP, CAP], BF16)
    h_sb = at("h_sb", U + 26624, [128, HCG, NGRP, CAP], BF16)
    w1_sb = [at(f"w1_sb{i}", U + 36864 + i * 16384, [128, NEIN, 1024], BF16)
             for i in range(2)]
    w2_sb = [at(f"w2_sb{i}", U + 69632 + i * 8192, [128, HCG, 512], BF16)
             for i in range(2)]
    y_acc = at("y_acc", U + 86016, [128, NGRP * 3, E], F32)
    y_bf = at("y_bf", U + 110592, [128, NGRP * 3, E], BF16)
    SelA = at("SelA", U + 122880, [128, NT, CAP], BF16)
    SelAs = at("SelAs", U + 128000, [128, NT, CAP], BF16)
    SelBs = at("SelBs", U + 133120, [128, NGRP * 3, 512], BF16)
    FEND = U + 139264
    assert FEND <= SBTOP, (U, FEND)
    x_stage = at("x_stage", U + 36864, [128, 4, E], F32)
    xn_mp = [at(f"xn_mp{i}", U + 53248 + i * 8192, [128, NEIN, 512], BF16)
             for i in range(2)]
    K_sb = [at(f"K_sb{i}", U + 69632 + i * 8192, [128, NEIN, 512], BF16)
            for i in range(2)]
    Wa_sb = at("Wa_sb", U + 86016, [128, NEIN, E], BF16)
    E_sb = at("E_sb", U + 102400, [128, 2, 512], BF16)
    Q_sb = at("Q_sb", U + 104448, [128, NEIN, T], BF16)
    Wb_sb = at("Wb_sb", U + 53248, [128, NEIN, E], BF16)
    V_sb = at("V_sb", U + 69632 + 4096, [128, 2, E], BF16)
    PT_sb = at("PT_sb", U + 77824, [128, 2, CH], BF16)
    comb = at("comb", U + 78848, [128, NEIN, CH], BF16)
    xrel = at("xrel", U + 36864, [128, 2, E], F32)
    braw = at("braw", U + 36864, [8, 128], F32)
    gwraw = at("gwraw", U + 36864 + 1024, [8, E], F32)
    b1raw = at("b1raw", U + 36864 + 8192, [128, 256], F32)

    pb = [nc.alloc_psum_tensor(f"pb{i}", [128, 512], F32) for i in range(8)]
    def pbf(i):
        return pb[i].ap().bitcast(BF16)

    sch = Sched()
    s_sp = sch.sem("s_sp")
    s_gp = sch.sem("s_gp")
    s_gw = sch.sem("s_gw")
    s_pe = sch.sem("s_pe")
    s_ve = sch.sem("s_ve")
    s_ac = sch.sem("s_ac")
    s_st = sch.sem("s_st")
    ALLS = (s_sp, s_gp, s_gw, s_pe, s_ve, s_ac, s_st)
    ENGS = ("sp", "gp", "pe", "ve", "act")

    dbg_tensors = {}

    def dbg(name, ap, rows, cols):
        if not debug:
            return
        d = nc.dram_tensor("dbg_" + name, [rows, cols], ap.dtype,
                           kind="ExternalOutput").ap()
        dbg_tensors[name] = d
        snap = [(s, s.n) for s in ALLS if s.n > 0]
        sch.dma("sp", d[:, :], ap, waits=snap, inc=s_st)

    def barrier():
        snap = [(s, s.n) for s in ALLS if s.n > 0]
        for e in ENGS:
            sch.op(e, "wait", waits=snap)

    def w(*sems):
        return [(s, s.n) for s in sems]

    sch.op("act", "table_name", setname="natural_log_exp_and_others")
    sch.dma("sp", id_bf.ap()[:, :], ident_bf_d.ap()[:, :], inc=s_sp)
    sch.dma("sp", id_f32.ap()[:, :], ident_f32_d.ap()[:, :], inc=s_sp)
    sch.dma("sp", lt_sb.ap()[:, :], lt_d.ap()[:, :], inc=s_sp)
    sch.dma("sp", onesm.ap()[:, :], onesm_d.ap()[:, :], inc=s_sp)
    sch.dma("sp", iota1.ap()[:, :], iota1_d.ap()[:, :], inc=s_sp)
    sch.dma("gp", bo_row.ap()[:, :], bo_d.rearrange("(a e) -> a e", a=1), inc=s_gp)
    sch.dma("gp", gb_row.ap()[:, :], gb_d.rearrange("(a e) -> a e", a=1), inc=s_gp)
    sch.dma("gp", b2_all.ap()[:, :], b2_d[:, :], inc=s_gp)
    sch.dma("sp", b1raw.ap()[:, :], b1_d.rearrange("a (x c) -> (a x) c", x=16), inc=s_sp)
    sch.dma("gp", bv33_r.ap()[:, :], bv_d.rearrange("(a e) -> a e", a=1), inc=s_gp)
    sch.op("ve", "ts", waits=w(s_gp), out=bv33_r.ap()[:, :], in_=None,
           in0=bv33_r.ap()[:, :], s1=float(NCROSS + 1), s2=None,
           op0=ALU.mult, op1=None, inc=s_ve)
    sch.op("gp", "memset", ap=ones_r.ap()[:, :], val=1.0, inc=s_ve)
    sch.op("gp", "memset", ap=ones_r_f32_guard.ap()[:, :], val=1.0, inc=s_ve)
    sch.dma("sp", gwraw.ap()[:, :], gw_d[:, :], inc=s_sp)

    ldv = w(s_sp)
    for j in range(2):
        sch.op("pe", "tp", waits=ldv if j == 0 else (),
               out=pb[1].ap()[:, j * 128:(j + 1) * 128],
               in_=b1raw.ap()[:, j * 128:(j + 1) * 128],
               ident=id_f32.ap()[:, :], inc=s_pe if j == 1 else None)
    sch.op("ve", "copy", waits=w(s_pe), out=b1_c.ap()[:, :],
           in_=pb[1].ap()[:, 0:256], inc=s_ve)

    for vec, dst, scale in ((bq_d, sbq_c, QSC), (bk_d, bk_c, 1.0),
                            (bv_d, bv33_c, float(NCROSS + 1))):
        sch.dma("sp", braw.ap()[:, :], vec.rearrange("(p k) -> p k", p=8),
                waits=w(s_pe), inc=s_sp)
        sch.op("pe", "tp", waits=w(s_sp, s_ve),
               out=pb[0].ap()[:, 0:8], in_=braw.ap()[:, :],
               ident=id_f32.ap()[0:8, 0:8], inc=s_pe)
        if scale == 1.0:
            sch.op("ve", "copy", waits=w(s_pe), out=dst.ap()[:, :],
                   in_=pb[0].ap()[:, 0:8], inc=s_ve)
        else:
            sch.op("ve", "ts", waits=w(s_pe), out=dst.ap()[:, :],
                   in0=pb[0].ap()[:, 0:8], s1=float(scale), s2=None,
                   op0=ALU.mult, op1=None, inc=s_ve)

    for c in range(NEIN):
        sch.op("pe", "tp", waits=w(s_ve) if c == 0 else (),
               out=pb[2].ap()[:, c * 8:(c + 1) * 8],
               in_=gwraw.ap()[:, c * 128:(c + 1) * 128],
               ident=id_f32.ap()[0:8, 0:8],
               inc=s_pe if c == NEIN - 1 else None)
    sch.op("ve", "copy", waits=w(s_pe), out=gwT.ap()[:, 0:NEIN * NEXP],
           in_=pb[2].ap()[:, 0:NEIN * NEXP], inc=s_ve)

    def b1col(k):
        return (k % 2) * 128 + (k // 2)

    ln_state = {"tok_hist": [], "bank_hist": {i: [] for i in range(8)}}

    def ln_group(xaps, x_waits, dsts, tp_banks, tokaps):
        g = len(xaps)
        st = stat.ap()
        for t, xap in enumerate(xaps):
            sch.op("ve", "bnst", waits=x_waits if t == 0 else (),
                   out=st[:, t * 12:t * 12 + 6], in_=xap[:, 0:512])
            sch.op("ve", "bnst", out=st[:, t * 12 + 6:t * 12 + 12],
                   in_=xap[:, 512:1024],
                   inc=s_ve if t == g - 1 else None)
        ve_stats = s_ve.n
        for t in range(g):
            sch.op("ve", "bnag", waits=[(s_ve, ve_stats)] if t == 0 else (),
                   out=st[:, 48 + 2 * t:50 + 2 * t],
                   in_=st[:, t * 12:t * 12 + 12],
                   inc=s_ve if t == g - 1 else None)
        ve_var = s_ve.n
        for t in range(g):
            sch.op("ve", "ts", waits=[(s_ve, ve_var)] if t == 0 else (),
                   out=st[:, t * 12:t * 12 + 1], in_=None,
                   in0=st[:, 49 + 2 * t:50 + 2 * t], s1=1e-5, s2=None,
                   op0=ALU.add, op1=None)
        for t in range(g):
            sch.op("ve", "ts", out=st[:, 56 + t:57 + t], in_=None,
                   in0=st[:, 49 + 2 * t:50 + 2 * t], s1=-0.5, s2=1.5,
                   op0=ALU.mult, op1=ALU.add,
                   inc=s_ve if t == g - 1 else None)
        for it in range(4):
            vy = s_ve.n
            for t in range(g):
                sch.op("ve", "tt", waits=[(s_ve, vy)] if t == 0 else (),
                       out=st[:, t * 12 + 1:t * 12 + 2],
                       in0=st[:, 56 + t:57 + t], in1=st[:, 56 + t:57 + t],
                       top=ALU.mult, inc=s_ve if t == g - 1 else None)
            vy = s_ve.n
            for t in range(g):
                sch.op("ve", "tt", waits=[(s_ve, vy)] if t == 0 else (),
                       out=st[:, t * 12 + 2:t * 12 + 3],
                       in0=st[:, t * 12:t * 12 + 1],
                       in1=st[:, t * 12 + 1:t * 12 + 2],
                       top=ALU.mult, inc=s_ve if t == g - 1 else None)
            vy = s_ve.n
            for t in range(g):
                sch.op("ve", "ts", waits=[(s_ve, vy)] if t == 0 else (),
                       out=st[:, t * 12 + 3:t * 12 + 4], in_=None,
                       in0=st[:, t * 12 + 2:t * 12 + 3], s1=-0.5, s2=1.5,
                       op0=ALU.mult, op1=ALU.add,
                       inc=s_ve if t == g - 1 else None)
            vy = s_ve.n
            for t in range(g):
                sch.op("ve", "tt", waits=[(s_ve, vy)] if t == 0 else (),
                       out=st[:, 56 + t:57 + t],
                       in0=st[:, 56 + t:57 + t], in1=st[:, t * 12 + 3:t * 12 + 4],
                       top=ALU.mult, inc=s_ve if t == g - 1 else None)
        ve_rstd = s_ve.n
        for t in range(g):
            sch.op("ve", "stt", waits=[(s_ve, ve_rstd)] if t == 0 else (),
                   out=st[:, 60 + t:61 + t], in0=st[:, 48 + 2 * t:49 + 2 * t],
                   scalar=-1.0, in1=st[:, 56 + t:57 + t],
                   op0=ALU.mult, op1=ALU.mult,
                   inc=s_ve if t == g - 1 else None)
        ve_lnb = s_ve.n
        ve_x_last = None
        for t, xap in enumerate(xaps):
            xw = [(s_ve, ve_lnb)] if t == 0 else []
            th = ln_state["tok_hist"]
            if len(th) >= 2:
                xw.append((s_pe, th[-2]))
            sch.op("ve", "ts", waits=xw, out=tokaps[t], in_=None,
                   in0=xap, s1=st[:, 56 + t:57 + t], s2=st[:, 60 + t:61 + t],
                   op0=ALU.mult, op1=ALU.add, inc=s_ve)
            ve_x_last = s_ve.n
            bank = tp_banks[t % len(tp_banks)]
            pw = [(s_ve, s_ve.n)]
            bh = ln_state["bank_hist"][bank]
            if bh:
                pw.append((s_ve, bh[-1]))
            for j in range(NEIN):
                sch.op("pe", "tp", waits=pw if j == 0 else (),
                       out=pbf(bank)[:, j * 128:(j + 1) * 128],
                       in_=tokaps[t][:, j * 128:(j + 1) * 128],
                       ident=id_bf.ap()[:, :],
                       inc=s_pe if j == NEIN - 1 else None)
            ln_state["tok_hist"].append(s_pe.n)
            sch.op("ve", "copy", waits=w(s_pe), out=dsts[t],
                   in_=pbf(bank).rearrange("p (a b) -> p a b", a=NEIN)[:, :, 0:128],
                   inc=s_ve)
            ln_state["bank_hist"][bank].append(s_ve.n)
        return ve_stats, ve_x_last

    for p in range(NPASS):
        barrier()
        prev_read = None
        for grp in range(NT // 4):
            r0 = p * T + grp * 512
            lw = []
            if prev_read is not None:
                lw = [(s_ve, max(prev_read[0], prev_read[1]))]
            sch.dma("sp", x_stage.ap()[:, :, :],
                    xk[r0:r0 + 512, :].rearrange("(a k) e -> k a e", k=128),
                    waits=lw, inc=s_sp)
            xw = w(s_sp)
            xaps = [x_stage.ap()[:, t, :] for t in range(4)]
            dsts = [xn_fm.ap()[:, :, grp * 512 + t * 128: grp * 512 + (t + 1) * 128]
                    for t in range(4)]
            tokaps = [xn_tok.ap()[:, grp * 4 + t, :] for t in range(4)]
            prev_read = ln_group(xaps, xw, dsts, (5, 7), tokaps)
        L_x_last = max(prev_read[0], prev_read[1])
        L_ve_end = s_ve.n

        if p == 0:
            dbg("xn_fm", xn_fm.ap()[:, :, :].rearrange("p a b -> p (a b)"), 128, NEIN * T)
        ve_gcopy = {}
        for t in range(NT):
            bank = t % 2
            pw = []
            if t == 0:
                pw.append((s_ve, L_ve_end))
            if t >= 2:
                pw.append((s_ve, ve_gcopy[t - 2]))
            for ein in range(NEIN):
                sch.op("pe", "mm", waits=pw if ein == 0 else (),
                       out=pb[bank].ap()[:, 0:8],
                       lhsT=xn_fm.ap()[:, ein, t * 128:(t + 1) * 128],
                       rhs=gwT.ap()[:, ein * 8:(ein + 1) * 8],
                       start=(ein == 0), stop=False)
            sch.op("pe", "mm", out=pb[bank].ap()[:, 0:8],
                   lhsT=ones_r.ap()[:, 0:128], rhs=gb_row.ap()[:, :],
                   start=False, stop=True, inc=s_pe)
            sch.op("ve", "copy", waits=w(s_pe), out=glog.ap()[:, t, :],
                   in_=pb[bank].ap()[:, 0:8], inc=s_ve)
            ve_gcopy[t] = s_ve.n
        for f in range(8):
            sch.op("act", "act", waits=w(s_ve) if f == 0 else (),
                   out=gtmp.ap()[:, f:f + 1], in_=ones_r_f32_guard.ap()[:, 0:1],
                   func=AF.Identity, bias=None, scale=None, accum=None,
                   inc=s_ac)
        sch.op("act", "act",
               out=gexp.ap()[:, :, :].rearrange("p a b -> p (a b)"),
               in_=glog.ap()[:, :, :].rearrange("p a b -> p (a b)"),
               func=AF.Exp, bias=np.float32(-44.0), scale=None, accum=None,
               inc=s_ac)
        ac_gexp = s_ac.n
        sch.op("ve", "red", waits=[(s_ac, ac_gexp)],
               out=gsum.ap()[:, :], in_=gexp.ap()[:, :, :],
               rop=ALU.add, negate=False, inc=s_ve)
        sch.op("ve", "recip", waits=w(s_ve), out=grec.ap()[:, :],
               in_=gsum.ap()[:, :], inc=s_ve)
        for t in range(NT):
            sch.op("ve", "red", waits=w(s_ve) if t == 0 else (),
                   out=gstat.ap()[:, 3:4], in_=gexp.ap()[:, t, :],
                   rop=ALU.max, negate=False, inc=s_ve)
            sch.op("ve", "ts", waits=w(s_ve), out=gmask.ap()[:, :], in_=None,
                   in0=gexp.ap()[:, t, :], s1=gstat.ap()[:, 3:4], s2=None,
                   op0=ALU.is_ge, op1=None, inc=s_ve)
            sch.op("ve", "stt", waits=w(s_ve), out=gmask.ap()[:, :],
                   in0=gmask.ap()[:, :], scalar=-1e30, in1=gexp.ap()[:, t, :],
                   op0=ALU.mult, op1=ALU.add, inc=s_ve)
            sch.op("ve", "red", waits=w(s_ve), out=gstat.ap()[:, 4:5],
                   in_=gmask.ap()[:, :], rop=ALU.max, negate=False, inc=s_ve)
            sch.op("ve", "ts", waits=w(s_ve), out=gmask.ap()[:, :], in_=None,
                   in0=gexp.ap()[:, t, :], s1=gstat.ap()[:, 4:5], s2=None,
                   op0=ALU.is_ge, op1=None, inc=s_ve)
            sch.op("ve", "stt", waits=w(s_ve), out=gate.ap()[:, t, :],
                   in0=gexp.ap()[:, t, :],
                   scalar=grec.ap()[:, t:t + 1], in1=gmask.ap()[:, :],
                   op0=ALU.mult, op1=ALU.mult, inc=s_ve)
        ve_gate = s_ve.n

        sch.op("ve", "ts", waits=[(s_ve, ve_gate)],
               out=mask_bf.ap()[:, :, :].rearrange("p a b -> p (a b)"), in_=None,
               in0=gate.ap()[:, :, :].rearrange("p a b -> p (a b)"),
               s1=0.0, s2=None, op0=ALU.is_gt, op1=None, inc=s_ve)
        sch.op("ve", "ts",
               out=mask_f.ap()[:, :, :].rearrange("p a b -> p (a b)"), in_=None,
               in0=gate.ap()[:, :, :].rearrange("p a b -> p (a b)"),
               s1=0.0, s2=None, op0=ALU.is_gt, op1=None, inc=s_ve)
        ve_masks = s_ve.n
        for t4 in range(NT // 4):
            for t in range(4):
                tw = ()
                if t == 0:
                    tw = [(s_ve, ve_gate if t4 == 0 else s_ve.n)]
                sch.op("pe", "tp", waits=tw,
                       out=pb[7].ap()[0:8, t * 128:(t + 1) * 128],
                       in_=gate.ap()[:, t4 * 4 + t, :],
                       ident=id_f32.ap()[:, :],
                       inc=s_pe if t == 3 else None)
            sch.op("ve", "copy", waits=w(s_pe),
                   out=gateT.ap()[:, t4 * 4:(t4 + 1) * 4, :]
                   .rearrange("p a b -> p (a b)"),
                   in_=pb[7].ap()[0:8, :], inc=s_ve)
        ve_gateT = s_ve.n
        for g in range(NGRP):
            for t in range(4):
                base = (g * 4 + t) * 8
                for tp_ in range(t + 1):
                    pw = []
                    if g == 0 and t == 0 and tp_ == 0:
                        pw.append((s_ve, ve_masks))
                    sch.op("pe", "mm", waits=pw,
                           out=pb[0].ap()[:, base:base + 8],
                           lhsT=(lt_sb.ap()[:, :] if tp_ == t else onesm.ap()[:, :]),
                           rhs=mask_bf.ap()[:, g * 4 + tp_, :],
                           start=(tp_ == 0), stop=(tp_ == t),
                           inc=s_pe if (tp_ == t) else None)
        sch.op("ve", "copy", waits=w(s_pe), out=cum_sb.ap()[:, :],
               in_=pb[0].ap()[:, 0:NGRP * 4 * NEXP], inc=s_ve)
        ve_cum = s_ve.n

        if p == 0:
            dbg("gate", gate.ap()[:, :, :].rearrange("p a b -> p (a b)"), 128, NT * NEXP)
            dbg("cum", cum_sb.ap()[:, :], 128, NGRP * 4 * NEXP)
        w1_done = {}
        w2_done = {}
        h_done_pe = {}
        y_done_pe = {}

        def emit_w1(i, waits=()):
            e, hg = divmod(i, NHG)
            sch.dma("gp", w1_sb[i % 2].ap()[:, :, :],
                    w1_d[e, :, hg * 1024:(hg + 1) * 1024]
                    .rearrange("(c k) h -> k c h", k=128),
                    waits=waits, inc=s_gw)
            w1_done[i] = s_gw.n

        def emit_w2(j, waits=()):
            i, half = divmod(j, 2)
            e, hg = divmod(i, NHG)
            sch.dma("gp", w2_sb[j % 2].ap()[:, :, :],
                    w2_d[e, hg * 1024:(hg + 1) * 1024,
                         half * 512:(half + 1) * 512]
                    .rearrange("(c k) o -> k c o", k=128),
                    waits=waits, inc=s_gw)
            w2_done[j] = s_gw.n

        emit_w1(0, waits=[(s_ve, L_x_last)])
        emit_w1(1)
        emit_w2(0)
        emit_w2(1)

        selA_done = {}
        selB_done = {}
        selB_pe = {}
        gather_done = {}
        gather_pe = {}
        ybf_done = {}
        scatter_ve = {}
        scatter_pe = {}
        hlast_ac = {}
        ylast_ve = {}
        h_hist = []
        b56_hist = []

        def build_sel(e):
            pw = [(s_ve, ve_cum)]
            if e >= 1:
                pw.append((s_pe, gather_pe[e - 1]))
                pw.append((s_pe, selB_pe[e - 1]))
            for ti in range(NT):
                col = ti * 8 + e
                sch.op("ve", "ts", waits=pw if ti == 0 else (),
                       out=SelA.ap()[:, ti, :], in_=None,
                       in0=iota1.ap()[:, :],
                       s1=cum_sb.ap()[:, col:col + 1],
                       s2=mask_f.ap()[:, ti, e:e + 1],
                       op0=ALU.is_equal, op1=ALU.mult)
                sch.op("ve", "ts",
                       out=SelAs.ap()[:, ti, :], in_=None,
                       in0=iota1.ap()[:, :],
                       s1=cum_sb.ap()[:, col:col + 1],
                       s2=gate.ap()[:, ti, e:e + 1],
                       op0=ALU.is_equal, op1=ALU.mult,
                       inc=s_ve if ti == NT - 1 else None)
            selA_done[e] = s_ve.n

        def build_selB(e):
            first = True
            for g in range(NGRP):
                for ci, (c0, cw) in enumerate(CC):
                    bank = 1 + (g * 3 + ci) % 2
                    pw = []
                    if first:
                        pw.append((s_ve, selA_done[e]))
                        if e >= 1:
                            pw.append((s_ve, scatter_ve[e - 1]))
                            pw.append((s_ac, gather_done[e - 1]))
                    for ti in range(4):
                        sch.op("pe", "tp", waits=pw if ti == 0 else (),
                               out=pbf(bank)[0:cw, ti * 128:(ti + 1) * 128],
                               in_=SelAs.ap()[:, g * 4 + ti, c0:c0 + cw],
                               ident=id_bf.ap()[:, :],
                               inc=s_pe if ti == 3 else None)
                        first = False
                    sch.op("ve", "copy", waits=w(s_pe),
                           out=SelBs.ap()[0:cw, g * 3 + ci, :],
                           in_=pbf(bank)[0:cw, 0:512], inc=s_ve)
            selB_done[e] = s_ve.n
            selB_pe[e] = s_pe.n

        def gather(e):
            ngt = 0
            for fc in range(NEIN):
                for g in range(NGRP):
                    bank = 1 + ngt % 2
                    pw = []
                    if ngt == 0:
                        pw.append((s_ve, selA_done[e]))
                        pw.append((s_ve, selB_done[e]))
                        if e >= 1:
                            pw.append((s_pe, h_done_pe[(e - 1) * NHG + NHG - 1]))
                    elif ngt >= 2:
                        pw.append((s_ac, gather_done.get((e, ngt - 2), 0)))
                    for ti in range(4):
                        sch.op("pe", "mm", waits=pw if ti == 0 else (),
                               out=pb[bank].ap()[:, 0:CAP],
                               lhsT=xn_tok.ap()[:, g * 4 + ti, fc * 128:(fc + 1) * 128],
                               rhs=SelA.ap()[:, g * 4 + ti, :],
                               start=(ti == 0), stop=(ti == 3),
                               inc=s_pe if ti == 3 else None)
                    sch.op("act", "act", waits=w(s_pe),
                           out=Xg.ap()[:, fc, g, :], in_=pb[bank].ap()[:, 0:CAP],
                           func=AF.Identity, bias=None, scale=None, accum=None,
                           inc=s_ac)
                    gather_done[(e, ngt)] = s_ac.n
                    ngt += 1
            gather_pe[e] = s_pe.n
            gather_done[e] = s_ac.n

        for e in range(NEXP):
            build_sel(e)
            build_selB(e)
            gather(e)
            for hg in range(NHG):
                i = e * NHG + hg
                for hc in range(HCG):
                    for g in range(NGRP):
                        nh = len(h_hist)
                        bank = 3 + nh % 2
                        pw = []
                        if hc == 0 and g == 0:
                            pw.append((s_gw, w1_done[i]))
                            pw.append((s_ac, gather_done[e]))
                        if nh >= 2:
                            pw.append((s_ac, h_hist[nh - 2]))
                        for kc in range(NEIN):
                            sch.op("pe", "mm", waits=pw if kc == 0 else (),
                                   out=pb[bank].ap()[:, 0:CAP],
                                   lhsT=w1_sb[i % 2].ap()[:, kc, hc * 128:(hc + 1) * 128],
                                   rhs=Xg.ap()[:, kc, g, :],
                                   start=(kc == 0), stop=(kc == NEIN - 1),
                                   inc=s_pe if kc == NEIN - 1 else None)
                        sch.op("act", "act", waits=w(s_pe),
                               out=h_sb.ap()[:, hc, g, :],
                               in_=pb[bank].ap()[:, 0:CAP], func=AF.Relu,
                               bias=b1_c.ap()[:, b1col(e * 32 + hg * 8 + hc):
                                              b1col(e * 32 + hg * 8 + hc) + 1],
                               scale=None, accum=None, inc=s_ac)
                        h_hist.append(s_ac.n)
                h_done_pe[i] = s_pe.n
                hlast_ac[i] = s_ac.n
                if i + 2 < NEXP * NHG:
                    emit_w1(i + 2, waits=[(s_pe, h_done_pe[i])])
                for half in range(2):
                    j = i * 2 + half
                    for g in range(NGRP):
                        for ci, (c0, cw) in enumerate(CC):
                            nb = len(b56_hist)
                            bank = 5 + nb % 2
                            pw = []
                            if g == 0 and ci == 0:
                                if half == 0:
                                    pw.append((s_ac, hlast_ac[i]))
                                pw.append((s_gw, w2_done[j]))
                            if nb >= 2:
                                pw.append((s_ve, b56_hist[nb - 2]))
                            for kc in range(HCG):
                                sch.op("pe", "mm", waits=pw if kc == 0 else (),
                                       out=pb[bank].ap()[0:cw, :],
                                       lhsT=h_sb.ap()[:, kc, g, c0:c0 + cw],
                                       rhs=w2_sb[j % 2].ap()[:, kc, :],
                                       start=(kc == 0), stop=(kc == HCG - 1),
                                       inc=s_pe if kc == HCG - 1 else None)
                            dst = y_acc.ap()[0:cw, g * 3 + ci,
                                             half * 512:(half + 1) * 512]
                            if hg == 0:
                                sch.op("ve", "copy", waits=w(s_pe),
                                       out=dst, in_=pb[bank].ap()[0:cw, :],
                                       inc=s_ve)
                            else:
                                sch.op("ve", "stt", waits=w(s_pe),
                                       out=dst, in0=pb[bank].ap()[0:cw, :],
                                       scalar=1.0, in1=dst,
                                       op0=ALU.mult, op1=ALU.add, inc=s_ve)
                            b56_hist.append(s_ve.n)
                    y_done_pe[j] = s_pe.n
                    if j + 2 < NEXP * NHG * 2:
                        emit_w2(j + 2, waits=[(s_pe, y_done_pe[j])])
                ylast_ve[i] = s_ve.n
            for g in range(NGRP):
                for ci, (c0, cw) in enumerate(CC):
                    pw = [(s_ve, ylast_ve[e * NHG + NHG - 1])] if (g == 0 and ci == 0) else []
                    if e >= 1 and g == 0 and ci == 0:
                        pw.append((s_pe, scatter_pe[e - 1]))
                    sch.op("ve", "copy", waits=pw,
                           out=y_bf.ap()[0:cw, g * 3 + ci, :],
                           in_=y_acc.ap()[0:cw, g * 3 + ci, :],
                           inc=s_ve if (g == NGRP - 1 and ci == 2) else None)
            ybf_done[e] = s_ve.n
            nsct = 0
            for ti in range(NT):
                g = ti // 4
                for half in range(2):
                    nb = len(b56_hist)
                    bank = 5 + nb % 2
                    pw = []
                    if nsct == 0:
                        pw.append((s_ve, ybf_done[e]))
                        pw.append((s_ve, selB_done[e]))
                    if nb >= 2:
                        pw.append((s_ve, b56_hist[nb - 2]))
                    if e == 0:
                        sch.op("pe", "mm", waits=pw,
                               out=pb[bank].ap()[:, :],
                               lhsT=gateT.ap()[:, ti, :],
                               rhs=b2_all.ap()[:, half * 512:(half + 1) * 512],
                               start=True, stop=False)
                        pw = ()
                    for ci, (c0, cw) in enumerate(CC):
                        sch.op("pe", "mm", waits=pw if ci == 0 else (),
                               out=pb[bank].ap()[:, :],
                               lhsT=SelBs.ap()[0:cw, g * 3 + ci,
                                               (ti % 4) * 128:(ti % 4 + 1) * 128],
                               rhs=y_bf.ap()[0:cw, g * 3 + ci,
                                             half * 512:(half + 1) * 512],
                               start=(e > 0 and ci == 0),
                               stop=(ci == 2),
                               inc=s_pe if ci == 2 else None)
                        pw = ()
                    dst = out_acc.ap()[:, ti, half * 512:(half + 1) * 512]
                    if e == 0:
                        sch.op("ve", "copy", waits=w(s_pe),
                               out=dst, in_=pb[bank].ap()[:, :], inc=s_ve)
                    else:
                        sch.op("ve", "stt", waits=w(s_pe),
                               out=dst, in0=pb[bank].ap()[:, :],
                               scalar=1.0, in1=dst,
                               op0=ALU.mult, op1=ALU.add, inc=s_ve)
                    b56_hist.append(s_ve.n)
                    nsct += 1
            scatter_ve[e] = s_ve.n
            scatter_pe[e] = s_pe.n

        if p == 0:
            dbg("out_acc", out_acc.ap()[:, :, :].rearrange("p a b -> p (a b)"), 128, NT * E)
        barrier()
        sch.op("act", "table_name", setname="silu_and_others")
        for t in range(NT):
            sw = [(s_ve, scatter_ve[NEXP - 1])] if t == 0 else ()
            sch.op("act", "act", waits=sw,
                   out=g_sb.ap()[:, t, :], in_=out_acc.ap()[:, t, :],
                   func=AF.Silu, bias=None, scale=None, accum=None,
                   inc=s_ac if t == NT - 1 else None)
        sch.op("act", "table_name", setname="natural_log_exp_and_others")

        if p == 0:
            dbg("g", g_sb.ap()[:, :, :].rearrange("p a b -> p (a b)"), 128, NT * E)
        barrier()
        sch.dma("gp", Wa_sb.ap()[:, :, :],
                Wq_d.rearrange("(c k) o -> k c o", k=128),
                waits=[(s_ve, ybf_done[NEXP - 1])], inc=s_gw)
        wq_done = s_gw.n
        acq_t4 = {}
        nq = 0
        for t4 in range(NT4):
            for eo in range(NEIN):
                bank = nq % 2
                pw = []
                if nq == 0:
                    pw.append((s_gw, wq_done))
                    pw.append((s_ac, gather_done[NEXP - 1]))
                if nq >= 2:
                    pw.append((s_ac, acq_t4.get(("h", nq - 2), 0)))
                for ein in range(NEIN):
                    sch.op("pe", "mm", waits=pw if ein == 0 else (),
                           out=pb[bank].ap()[:, :],
                           lhsT=Wa_sb.ap()[:, ein, eo * 128:(eo + 1) * 128],
                           rhs=xn_fm.ap()[:, ein, t4 * 512:(t4 + 1) * 512],
                           start=(ein == 0), stop=(ein == NEIN - 1),
                           inc=s_pe if ein == NEIN - 1 else None)
                ew = [(s_pe, s_pe.n)]
                if nq == 0:
                    ew.append((s_ve, ybf_done[NEXP - 1]))
                    ew.append((s_pe, scatter_pe[NEXP - 1]))
                sch.op("act", "act", waits=ew,
                       out=Q_sb.ap()[:, eo, t4 * 512:(t4 + 1) * 512],
                       in_=pb[bank].ap()[:, :], func=AF.Identity,
                       bias=sbq_c.ap()[:, eo:eo + 1], scale=np.float32(QSC),
                       accum=None, inc=s_ac)
                acq_t4[("h", nq)] = s_ac.n
                nq += 1
            acq_t4[t4] = s_ac.n

        barrier()
        if p == 0:
            sch.dma("gp", Wa_sb.ap()[:, :, :],
                    Wk_d.rearrange("(c k) o -> k c o", k=128),
                    waits=w(s_pe), inc=s_gw)
            wk_done = s_gw.n
        prev_ln_read = None
        score_last = {}
        kevac = {}
        spill_done = {}
        stt_hist = []
        nsc = 0
        for mp in range(NMP):
            buf = mp % 2
            kq = ("sp", s_sp) if mp % 2 == 0 else ("gp", s_gw)
            if p == 0:
                if prev_ln_read is not None:
                    lw = [(s_ve, max(prev_ln_read[0], prev_ln_read[1]))]
                else:
                    lw = [(s_ve, L_x_last),
                          (s_pe, h_done_pe[NEXP * NHG - 2])]
                sch.dma("sp", x_stage.ap()[:, :, :],
                        xk[mp * 512:(mp + 1) * 512, :]
                        .rearrange("(a k) e -> k a e", k=128),
                        waits=lw, inc=s_sp)
                xw = w(s_sp)
                if mp == 0:
                    xw = xw + [(s_pe, gather_pe[NEXP - 1])]
                xaps = [x_stage.ap()[:, t, :] for t in range(4)]
                dsts = [xn_mp[buf].ap()[:, :, t * 128:(t + 1) * 128] for t in range(4)]
                tokaps = [xn_tok.ap()[:, t % 2, :] for t in range(4)]
                prev_ln_read = ln_group(xaps, xw, dsts, (7,), tokaps)
                xnmp_evac = s_ve.n
                for eo in range(NEIN):
                    bank = eo % 2
                    pw = []
                    if eo == 0:
                        pw.append((s_ve, xnmp_evac))
                        if mp == 0:
                            pw.append((s_gw, wk_done))
                    if eo >= 2:
                        pw.append((s_ac, kevac.get((buf, eo - 2), 0)))
                    for ein in range(NEIN):
                        sch.op("pe", "mm", waits=pw if ein == 0 else (),
                               out=pb[bank].ap()[:, :],
                               lhsT=Wa_sb.ap()[:, ein, eo * 128:(eo + 1) * 128],
                               rhs=xn_mp[buf].ap()[:, ein, :],
                               start=(ein == 0), stop=(ein == NEIN - 1),
                               inc=s_pe if ein == NEIN - 1 else None)
                    ew = [(s_pe, s_pe.n)]
                    if mp >= 2:
                        ew.append((s_pe, score_last.get(buf, 0)))
                        ew.append(spill_done[mp - 2])
                    sch.op("act", "act", waits=ew,
                           out=K_sb[buf].ap()[:, eo, :], in_=pb[bank].ap()[:, :],
                           func=AF.Identity, bias=bk_c.ap()[:, eo:eo + 1],
                           scale=None, accum=None, inc=s_ac)
                    kevac[(buf, eo)] = s_ac.n
                kevac_all = s_ac.n
                sch.dma(kq[0], kspill_d[mp], K_sb[buf].ap()[:, :, :],
                        waits=[(s_ac, kevac_all)], inc=kq[1])
                spill_done[mp] = (kq[1], kq[1].n)
                kw0 = [(s_ac, kevac_all)]
            else:
                lw = []
                if mp >= 2:
                    lw.append((s_pe, score_last.get(buf, 0)))
                else:
                    lw.append((s_pe, y_done_pe[NEXP * NHG * 2 - 1]))
                sch.dma(kq[0], K_sb[buf].ap()[:, :, :], kspill_d[mp],
                        waits=lw, inc=kq[1])
                spill_done[mp] = (kq[1], kq[1].n)
                kw0 = [spill_done[mp]]
            for n in range(NOWN):
                own_glob = p * NOWN + n
                for qt in range(2):
                    idx = nsc
                    bank = 2 + idx % 4
                    sbuf = idx % 2
                    pw = list(kw0) if (n == 0 and qt == 0) else []
                    if idx == 0:
                        pw.append((s_ac, gather_done[NEXP - 1]))
                        pw.append((s_ac, hlast_ac[NEXP * NHG - 1]))
                        pw.append((s_ve, scatter_ve[NEXP - 1]))
                    if qt == 0:
                        pw.append((s_ac, acq_t4[n // 2]))
                    if idx >= 4:
                        pw.append((s_ve, stt_hist[idx - 4]))
                    for ein in range(NEIN):
                        sch.op("pe", "mm", waits=pw if ein == 0 else (),
                               out=pb[bank].ap()[:, :],
                               lhsT=Q_sb.ap()[:, ein,
                                              n * 256 + qt * 128:n * 256 + (qt + 1) * 128],
                               rhs=K_sb[buf].ap()[:, ein, :],
                               start=(ein == 0), stop=(ein == NEIN - 1),
                               inc=s_pe if ein == NEIN - 1 else None)
                    pe_s = s_pe.n
                    rw = [(s_pe, pe_s)]
                    if idx >= 2:
                        rw.append((s_ve, stt_hist[idx - 2]))
                    for hh in range(2):
                        sch.op("act", "act", waits=rw if hh == 0 else (),
                               out=E_sb.ap()[:, sbuf, hh * 256:(hh + 1) * 256],
                               in_=pb[bank].ap()[:, hh * 256:(hh + 1) * 256],
                               func=AF.Exp, bias=np.float32(-44.0),
                               scale=None, accum=smx.ap()[:, sbuf, 2 + hh:3 + hh],
                               inc=s_ac if hh == 1 else None)
                    ac_e = s_ac.n
                    sch.op("ve", "recip", waits=[(s_ac, ac_e)],
                           out=smx.ap()[:, sbuf, 4:6], in_=smx.ap()[:, sbuf, 2:4],
                           inc=s_ve)
                    ve_rr = s_ve.n
                    dbl_half = None
                    if own_glob // 2 == mp:
                        dbl_half = own_glob % 2
                        sch.op("ve", "ts", waits=[(s_ve, ve_rr)], out=smx.ap()[:, sbuf, 6:7],
                               in_=None, in0=smx.ap()[:, sbuf, 4 + dbl_half:5 + dbl_half],
                               s1=2.0, s2=None, op0=ALU.mult, op1=None,
                               inc=s_ve)
                        ve_rr = s_ve.n
                    for hh in range(2):
                        sc = (smx.ap()[:, sbuf, 6:7] if hh == dbl_half
                              else smx.ap()[:, sbuf, 4 + hh:5 + hh])
                        if mp == 0 and hh == 0:
                            sch.op("ve", "ts", waits=[(s_ve, ve_rr)],
                                   out=P_acc.ap()[:, n, qt, :], in_=None,
                                   in0=E_sb.ap()[:, sbuf, hh * 256:(hh + 1) * 256],
                                   s1=sc, s2=None, op0=ALU.mult, op1=None,
                                   inc=s_ve)
                        else:
                            sch.op("ve", "stt",
                                   waits=[(s_ve, ve_rr if hh == 0 else s_ve.n)],
                                   out=P_acc.ap()[:, n, qt, :],
                                   in0=E_sb.ap()[:, sbuf, hh * 256:(hh + 1) * 256],
                                   scalar=sc, in1=P_acc.ap()[:, n, qt, :],
                                   op0=ALU.mult, op1=ALU.add,
                                   inc=s_ve)
                    stt_hist.append(s_ve.n)
                    nsc += 1
            score_last[buf] = s_pe.n

        if p == 0:
            dbg("P", P_acc.ap()[:, :, :, :].rearrange("p a b c -> p (a b c)"), 128, NOWN * 2 * CH)
        A4_dbg_pass = (p == 0)
        barrier()
        sch.dma("gp", Wa_sb.ap()[:, :, :],
                Wv_d.rearrange("(c k) o -> k c o", k=128), inc=s_gw)
        sch.dma("gp", Wb_sb.ap()[:, :, :],
                Wo_d.rearrange("(c k) o -> k c o", k=128), inc=s_gw)
        wvo_done = s_gw.n
        ac_v_hist = []
        pt_evac_last = 0
        ve_cmb_hist = []
        st_hist = []
        nv = 0
        nidx = 0
        for n in range(NOWN):
            for tt in range(2):
                for half in range(2):
                    bank = nv % 2
                    pw = []
                    if nv == 0:
                        pw.append((s_gw, wvo_done))
                    if nv >= 2:
                        pw.append((s_ac, ac_v_hist[nv - 2]))
                    for ein in range(NEIN):
                        sch.op("pe", "mm", waits=pw if ein == 0 else (),
                               out=pb[bank].ap()[:, :],
                               lhsT=xn_fm.ap()[:, ein,
                                               n * 256 + tt * 128:n * 256 + (tt + 1) * 128],
                               rhs=Wa_sb.ap()[:, ein, half * 512:(half + 1) * 512],
                               start=(ein == 0), stop=(ein == NEIN - 1),
                               inc=s_pe if ein == NEIN - 1 else None)
                    vw = [(s_pe, s_pe.n)]
                    sch.op("act", "act", waits=vw,
                           out=V_sb.ap()[:, tt, half * 512:(half + 1) * 512],
                           in_=pb[bank].ap()[:, :], func=AF.Identity, bias=None,
                           scale=None, accum=None, inc=s_ac)
                    ac_v_hist.append(s_ac.n)
                    nv += 1
            ac_v_all = s_ac.n
            ptw = []
            if pt_evac_last:
                ptw.append((s_ve, pt_evac_last))
            first = True
            for qt in range(2):
                for kt in range(2):
                    sch.op("pe", "tp", waits=ptw if first else (),
                           out=pb[2].ap()[:, kt * 256 + qt * 128:kt * 256 + (qt + 1) * 128],
                           in_=P_acc.ap()[:, n, qt, kt * 128:(kt + 1) * 128],
                           ident=id_f32.ap()[:, :],
                           inc=s_pe if (qt == 1 and kt == 1) else None)
                    first = False
            sch.op("ve", "copy", waits=w(s_pe), out=PT_sb.ap()[:, :, :],
                   in_=pb[2].ap().rearrange("p (a b) -> p a b", a=2), inc=s_ve)
            pt_evac_last = s_ve.n
            ve_comb = {}
            for ec in range(NEIN):
                bank = 3 + (ec // 2) % 2
                off = (ec % 2) * 256
                pw = []
                if ec == 0:
                    pw.append((s_ve, pt_evac_last))
                    pw.append((s_ac, ac_v_all))
                if ec % 2 == 0 and ec >= 4:
                    pw.append((s_ve, ve_comb[ec - 4]))
                for kt in range(2):
                    sch.op("pe", "mm", waits=pw if kt == 0 else (),
                           out=pb[bank].ap()[:, off:off + 256],
                           lhsT=V_sb.ap()[:, kt, ec * 128:(ec + 1) * 128],
                           rhs=PT_sb.ap()[:, kt, :],
                           start=(kt == 0), stop=False)
                sch.op("pe", "mm", out=pb[bank].ap()[:, off:off + 256],
                       lhsT=bv33_r.ap()[:, ec * 128:(ec + 1) * 128],
                       rhs=ones_r.ap()[:, 0:256],
                       start=False, stop=True,
                       inc=s_pe if ec % 2 == 1 else None)
                if ec % 2 == 1:
                    sch.op("ve", "copy", waits=w(s_pe),
                           out=comb.ap()[:, ec - 1:ec + 1, :]
                           .rearrange("p a b -> p (a b)"),
                           in_=pb[bank].ap()[:, :], inc=s_ve)
                    ve_comb[ec - 1] = s_ve.n
            comb_all_ve = s_ve.n
            for qt in range(2):
                obuf = nidx % 2
                xlw = []
                if len(ve_cmb_hist) >= 2:
                    xlw.append((s_ve, ve_cmb_hist[-2]))
                r0 = p * T + n * 256 + qt * 128
                sch.dma("sp", xrel.ap()[:, obuf, :], xk[r0:r0 + 128, :],
                        waits=xlw, inc=s_sp)
                sp_x = s_sp.n
                for half in range(2):
                    bank = 5 + (qt * 2 + half) % 2
                    pw = []
                    if qt == 0 and half == 0:
                        pw.append((s_ve, comb_all_ve))
                    if len(ve_cmb_hist) >= 2:
                        pw.append((s_ve, ve_cmb_hist[-2]))
                    for ec in range(NEIN):
                        sch.op("pe", "mm", waits=pw if ec == 0 else (),
                               out=pb[bank].ap()[:, :],
                               lhsT=comb.ap()[:, ec, qt * 128:(qt + 1) * 128],
                               rhs=Wb_sb.ap()[:, ec, half * 512:(half + 1) * 512],
                               start=(ec == 0), stop=False)
                    sch.op("pe", "mm", out=pb[bank].ap()[:, :],
                           lhsT=ones_r.ap()[:, 0:128],
                           rhs=bo_row.ap()[:, half * 512:(half + 1) * 512],
                           start=False, stop=True, inc=s_pe)
                    pe_sv = s_pe.n
                    t_glob = n * 2 + qt
                    cw = [(s_pe, pe_sv), (s_sp, sp_x)]
                    if len(st_hist) >= 2:
                        cw.append((s_st, st_hist[-2]))
                    sch.op("ve", "tt", waits=cw,
                           out=out_stg.ap()[:, obuf, half * 512:(half + 1) * 512],
                           in0=pb[bank].ap()[:, :],
                           in1=g_sb.ap()[:, t_glob, half * 512:(half + 1) * 512],
                           top=ALU.mult, inc=s_ve)
                    sch.op("ve", "tt", waits=w(s_ve),
                           out=out_stg.ap()[:, obuf, half * 512:(half + 1) * 512],
                           in0=out_stg.ap()[:, obuf, half * 512:(half + 1) * 512],
                           in1=xrel.ap()[:, obuf, half * 512:(half + 1) * 512],
                           top=ALU.add, inc=s_ve)
                    ve_cmb_hist.append(s_ve.n)
                ve_both = s_ve.n
                sch.dma("sp", out_d[r0:r0 + 128, :], out_stg.ap()[:, obuf, :],
                        waits=[(s_ve, ve_both)], inc=s_st)
                st_hist.append(s_st.n)
                nidx += 1
        if A4_dbg_pass:
            dbg("Vd", V_sb.ap()[:, :, :].rearrange("p a b -> p (a b)"), 128, 2 * E)

    sch.op("sp", "wait", waits=[(s_st, s_st.n)])

    for r in sch.recs:
        for sem, v in r.waits:
            assert v <= sem.n, f"unreachable wait {sem.name} {v} > {sem.n}"

    _emit(nc, sch)
    return nc


def _emit(nc, sch):
    from contextlib import ExitStack
    with ExitStack() as stack:
        for s in sch.sems:
            s.h = stack.enter_context(nc.semaphore(s.name))
        blk = stack.enter_context(nc.Block())
        last_wait = {}

        def emit_for(eng_name, eng_obj, is_queue=False):
            for r in sch.recs:
                if r.eng != eng_name:
                    continue
                for sem, v in r.waits:
                    key = (eng_name, sem.name)
                    if last_wait.get(key, 0) >= v:
                        continue
                    eng_obj.wait_ge(sem.h, v)
                    last_wait[key] = v
                a = r.args
                k = r.kind
                if k == "wait":
                    inst = None
                elif k == "dma":
                    inst = eng_obj.dma_start(out=a["out"], in_=a["in_"])
                elif k == "memset":
                    inst = eng_obj.memset(a["ap"], a["val"])
                elif k == "mm":
                    inst = eng_obj.matmul(a["out"], a["lhsT"], a["rhs"],
                                          start=a["start"], stop=a["stop"])
                elif k == "tp":
                    inst = eng_obj.transpose(a["out"], a["in_"], a["ident"])
                elif k == "act":
                    kw = {}
                    if a.get("bias") is not None:
                        kw["bias"] = a["bias"]
                    if a.get("scale") is not None:
                        kw["scale"] = a["scale"]
                    if a.get("accum") is not None:
                        kw["accum_out"] = a["accum"]
                    inst = eng_obj.activation(a["out"], a["in_"], a["func"], **kw)
                elif k == "table":
                    inst = eng_obj.preload_act_table(a["func"])
                elif k == "table_name":
                    from concourse.hw_specs import get_activation_tables
                    _tabs = list(get_activation_tables(nc.m.arch).keys())
                    inst = eng_obj.add_instruction(mybir.InstLoadActFuncSet(
                        name=nc.get_next_instruction_name(),
                        act_func_set_id=_tabs.index(a["setname"]),
                        ins=[], outs=[]))
                elif k == "red":
                    inst = eng_obj.tensor_reduce(a["out"], a["in_"], axis=AX.X,
                                                 op=a["rop"], negate=a["negate"])
                elif k == "tt":
                    inst = eng_obj.tensor_tensor(a["out"], a["in0"], a["in1"],
                                                 op=a["top"])
                elif k == "ts":
                    if a.get("op1") is None:
                        inst = eng_obj.tensor_scalar(a["out"], a["in0"], a["s1"],
                                                     None, a["op0"])
                    else:
                        inst = eng_obj.tensor_scalar(a["out"], a["in0"], a["s1"],
                                                     a["s2"], a["op0"], a["op1"])
                elif k == "stt":
                    inst = eng_obj.scalar_tensor_tensor(
                        a["out"], a["in0"], a["scalar"], a["in1"],
                        op0=a["op0"], op1=a["op1"])
                elif k == "bnst":
                    inst = eng_obj.bn_stats(a["out"], a["in_"])
                elif k == "bnag":
                    inst = eng_obj.bn_aggr(a["out"], a["in_"])
                elif k == "recip":
                    inst = eng_obj.reciprocal(a["out"], a["in_"])
                elif k == "copy":
                    inst = eng_obj.tensor_copy(a["out"], a["in_"])
                else:
                    raise ValueError(k)
                for sem, kk in r.incs:
                    assert inst is not None, f"inc on wait record {k}"
                    inst.then_inc(sem.h, kk)

        @blk.sync
        def _(sync):
            emit_for("sp", sync, True)

        @blk.gpsimd
        def _(gp):
            emit_for("gp", gp, True)

        @blk.tensor
        def _(te):
            emit_for("pe", te)

        @blk.vector
        def _(ve):
            emit_for("ve", ve)

        @blk.scalar
        def _(se):
            emit_for("act", se)



def kernel(L0_attn_x, L0_attn_Wq, L0_attn_bq, L0_attn_Wk, L0_attn_bk,
           L0_attn_Wv, L0_attn_bv, L0_attn_Wo, L0_attn_bo,
           L0_ffn_gate_w, L0_ffn_gate_b, L0_ffn_w1, L0_ffn_b1,
           L0_ffn_w2, L0_ffn_b2):
    x = np.ascontiguousarray(np.asarray(L0_attn_x, dtype=np.float32))
    weights = {
        "Wq": L0_attn_Wq, "bq": L0_attn_bq, "Wk": L0_attn_Wk, "bk": L0_attn_bk,
        "Wv": L0_attn_Wv, "bv": L0_attn_bv, "Wo": L0_attn_Wo, "bo": L0_attn_bo,
        "gw": L0_ffn_gate_w, "gb": L0_ffn_gate_b,
        "w1": L0_ffn_w1, "b1": L0_ffn_b1, "w2": L0_ffn_w2, "b2": L0_ffn_b2,
    }
    weights = {k: np.ascontiguousarray(np.asarray(v, dtype=np.float32))
               for k, v in weights.items()}

    nc = build_program()
    in_maps = []
    for c in range(8):
        b, h = divmod(c, 2)
        xb = x[b]
        xkc = np.ascontiguousarray(
            np.concatenate([xb[h * 4096:], xb[:h * 4096]], axis=0))
        m = {"xk": xkc}
        m.update(weights)
        in_maps.append(m)
    res = run_bass_kernel_spmd(nc, in_maps, core_ids=list(range(8)))
    out = np.empty((4, 8192, 1024), np.float32)
    for c in range(8):
        b, h = divmod(c, 2)
        out[b, h * 4096:(h + 1) * 4096] = res.results[c]["out"]
    return out


# version 16 (final)
# speedup vs baseline: 1.0258x; 1.0258x over previous
import numpy as np
import ml_dtypes

import concourse.bass as bass
import concourse.mybir as mybir
from concourse.bass_utils import run_bass_kernel_spmd

F32 = mybir.dt.float32
BF16 = mybir.dt.bfloat16
AF = mybir.ActivationFunctionType
ALU = mybir.AluOpType
AX = mybir.AxisListType



class Sem:
    __slots__ = ("name", "n", "h")
    def __init__(self, name):
        self.name, self.n, self.h = name, 0, None


class Rec:
    __slots__ = ("eng", "kind", "args", "waits", "incs")
    def __init__(self, eng, kind, args, waits, incs):
        self.eng, self.kind, self.args, self.waits, self.incs = \
            eng, kind, args, waits, incs


class Sched:
    def __init__(self):
        self.recs = []
        self.sems = []

    def sem(self, name):
        s = Sem(name)
        self.sems.append(s)
        return s

    def op(self, eng, kind, waits=(), inc=None, inc_k=1, **args):
        for sem, v in waits:
            assert v <= sem.n, f"wait {sem.name}>={v} not yet scheduled ({sem.n})"
        incs = []
        if inc is not None:
            inc.n += inc_k
            incs.append((inc, inc_k))
        self.recs.append(Rec(eng, kind, args, [(s, v) for s, v in waits if v > 0], incs))
        return self.recs[-1]

    def dma(self, queue, out, in_, waits=(), inc=None):
        return self.op(queue, "dma", waits=waits, inc=inc, inc_k=16, out=out, in_=in_)



def build_program(S=8192, T=1024, NPASS=4, debug=False):
    E, CH, NEXP, H = 1024, 256, 8, 4096
    NEIN = E // 128
    NT = T // 128
    NT4 = T // 512
    NMP = (S // CH) // 2
    NOWN = T // CH
    NHG = 4
    HCG = 8
    NGRP = T // 512
    CAP = 320
    CC = ((0, 128), (128, 128), (256, 64))
    QSC = float(1.0 / np.sqrt(E))
    NCROSS = S // CH
    assert T % 512 == 0 and S % 512 == 0

    nc = bass.Bass("TRN2", target_bir_lowering=False, debug=False)

    xk = nc.dram_tensor("xk", [S, E], F32, kind="ExternalInput").ap()
    Wq_d = nc.dram_tensor("Wq", [E, E], F32, kind="ExternalInput").ap()
    bq_d = nc.dram_tensor("bq", [E], F32, kind="ExternalInput").ap()
    Wk_d = nc.dram_tensor("Wk", [E, E], F32, kind="ExternalInput").ap()
    bk_d = nc.dram_tensor("bk", [E], F32, kind="ExternalInput").ap()
    Wv_d = nc.dram_tensor("Wv", [E, E], F32, kind="ExternalInput").ap()
    bv_d = nc.dram_tensor("bv", [E], F32, kind="ExternalInput").ap()
    Wo_d = nc.dram_tensor("Wo", [E, E], F32, kind="ExternalInput").ap()
    bo_d = nc.dram_tensor("bo", [E], F32, kind="ExternalInput").ap()
    gw_d = nc.dram_tensor("gw", [NEXP, E], F32, kind="ExternalInput").ap()
    gb_d = nc.dram_tensor("gb", [NEXP], F32, kind="ExternalInput").ap()
    w1_d = nc.dram_tensor("w1", [NEXP, E, H], F32, kind="ExternalInput").ap()
    b1_d = nc.dram_tensor("b1", [NEXP, H], F32, kind="ExternalInput").ap()
    w2_d = nc.dram_tensor("w2", [NEXP, H, E], F32, kind="ExternalInput").ap()
    b2_d = nc.dram_tensor("b2", [NEXP, E], F32, kind="ExternalInput").ap()
    out_d = nc.dram_tensor("out", [NPASS * T, E], F32, kind="ExternalOutput").ap()
    kspill_d = nc.dram_tensor("kspill", [NMP, 128, NEIN, 512], BF16,
                              kind="Internal").ap()

    ident_bf_d = nc.inline_tensor(np.eye(128, dtype=ml_dtypes.bfloat16), "ident_bf")
    ident_f32_d = nc.inline_tensor(np.eye(128, dtype=np.float32), "ident_f32")
    lt_d = nc.inline_tensor(
        np.triu(np.ones((128, 128), np.float32)).astype(ml_dtypes.bfloat16), "lt_inc")
    onesm_d = nc.inline_tensor(np.ones((128, 128), dtype=ml_dtypes.bfloat16), "ones_m")
    iota1_d = nc.inline_tensor(
        np.broadcast_to(np.arange(1, CAP + 1, dtype=np.float32), (128, CAP)).copy(),
        "iota1")

    def at(name, off, shape, dtype):
        return nc.alloc_sbuf_tensor_at(name, shape, dtype, offset=off)

    SBBASE = 16384 + 2048
    SBTOP = 229216
    cur = [SBBASE]
    def nxt(name, shape, dtype, align=64):
        elt = 2 if dtype == BF16 else 4
        n = 1
        for d in shape[1:]:
            n *= d
        cur[0] = (cur[0] + align - 1) // align * align
        t = at(name, cur[0], shape, dtype)
        cur[0] += n * elt
        return t

    xn_fm = nxt("xn_fm", [128, NEIN, T], BF16)
    gate = nxt("gate", [128, NT, NEXP], F32)
    sbq_c = nxt("sbq_c", [128, 8], F32)
    bk_c = nxt("bk_c", [128, 8], F32)
    bv33_c = nxt("bv33_c", [128, 8], F32)
    b1_c = nxt("b1_c", [128, 256], F32)
    b2_all = nxt("b2_all", [8, E], BF16)
    bo_row = nxt("bo_row", [1, E], BF16)
    gb_row = nxt("gb_row", [1, NEXP], BF16)
    gwT = nxt("gwT", [128, NEIN * NEXP], BF16)
    ones_r = nxt("ones_r", [1, 256], BF16)
    bv33_r = nxt("bv33_r", [1, E], BF16)
    glog = nxt("glog", [128, NT, NEXP], F32)
    gexp = nxt("gexp", [128, NT, NEXP], F32)
    gsum = nxt("gsum", [128, NT], F32)
    grec = nxt("grec", [128, NT], F32)
    id_bf = nxt("id_bf", [128, 128], BF16)
    id_f32 = nxt("id_f32", [128, 128], F32)
    lt_sb = nxt("lt_sb", [128, 128], BF16)
    onesm = nxt("onesm", [128, 128], BF16)
    iota1 = nxt("iota1", [128, CAP], F32)
    gateT = nxt("gateT", [8, NT, 128], BF16)
    cum_sb = nxt("cum_sb", [128, NGRP * 4 * NEXP], F32)
    mask_bf = nxt("mask_bf", [128, NT, NEXP], BF16)
    mask_f = nxt("mask_f", [128, NT, NEXP], F32)
    stat = nxt("stat", [128, 64], F32)
    smx = nxt("smx", [128, 2, 8], F32)
    gstat = nxt("gstat", [128, 8], F32)
    gtmp = nxt("gtmp", [128, 8], F32)
    gmask = nxt("gmask", [128, 8], F32)
    ones_r_f32_guard = nxt("guard_src", [128, 1], F32)
    PERS_END = (cur[0] + 63) // 64 * 64

    A0 = PERS_END
    out_acc = at("out_acc", A0, [128, NT, E], F32)
    g_sb = at("g_sb", A0, [128, NT, E], BF16)
    P_acc = at("P_acc", A0 + NT * E * 2, [128, NOWN, 2, CH], F32)
    out_stg = at("out_stg", A0 + NT * E * 2 + NOWN * 2 * CH * 4, [128, 2, E], F32)
    assert NT * E * 2 + NOWN * 2 * CH * 4 + 2 * E * 4 <= NT * E * 4

    U = A0 + NT * E * 4
    xn_tok = at("xn_tok", U, [128, NT, E], BF16)
    Xg = at("Xg", U + 16384, [128, NEIN, NGRP, CAP], BF16)
    h_sb = at("h_sb", U + 26624, [128, HCG, NGRP, CAP], BF16)
    w1_sb = [at(f"w1_sb{i}", U + 36864 + i * 16384, [128, NEIN, 1024], BF16)
             for i in range(2)]
    w2_sb = [at(f"w2_sb{i}", U + 69632 + i * 8192, [128, HCG, 512], BF16)
             for i in range(2)]
    y_acc = at("y_acc", U + 86016, [128, NGRP * 3, E], F32)
    y_bf = at("y_bf", U + 110592, [128, NGRP * 3, E], BF16)
    SelA = at("SelA", U + 122880, [128, NT, CAP], BF16)
    SelAs = at("SelAs", U + 128000, [128, NT, CAP], BF16)
    SelBs = at("SelBs", U + 133120, [128, NGRP * 3, 512], BF16)
    FEND = U + 139264
    assert FEND <= SBTOP, (U, FEND)
    x_stage = at("x_stage", U + 36864, [128, 4, E], F32)
    xn_mp = [at(f"xn_mp{i}", U + 53248 + i * 8192, [128, NEIN, 512], BF16)
             for i in range(2)]
    K_sb = [at(f"K_sb{i}", U + 69632 + i * 8192, [128, NEIN, 512], BF16)
            for i in range(2)]
    Wa_sb = at("Wa_sb", U + 86016, [128, NEIN, E], BF16)
    E_sb = at("E_sb", U + 102400, [128, 2, 512], BF16)
    Q_sb = at("Q_sb", U + 104448, [128, NEIN, T], BF16)
    Wb_sb = at("Wb_sb", U + 53248, [128, NEIN, E], BF16)
    V_sb = at("V_sb", U + 69632 + 4096, [128, 2, E], BF16)
    PT_sb = at("PT_sb", U + 77824, [128, 2, CH], BF16)
    comb = at("comb", U + 78848, [128, NEIN, CH], BF16)
    xrel = at("xrel", U + 36864, [128, 2, E], F32)
    braw = at("braw", U + 36864, [8, 128], F32)
    gwraw = at("gwraw", U + 36864 + 1024, [8, E], F32)
    b1raw = at("b1raw", U + 36864 + 8192, [128, 256], F32)

    pb = [nc.alloc_psum_tensor(f"pb{i}", [128, 512], F32) for i in range(8)]
    def pbf(i):
        return pb[i].ap().bitcast(BF16)

    sch = Sched()
    s_sp = sch.sem("s_sp")
    s_gp = sch.sem("s_gp")
    s_gw = sch.sem("s_gw")
    s_pe = sch.sem("s_pe")
    s_ve = sch.sem("s_ve")
    s_ac = sch.sem("s_ac")
    s_st = sch.sem("s_st")
    ALLS = (s_sp, s_gp, s_gw, s_pe, s_ve, s_ac, s_st)
    ENGS = ("sp", "gp", "pe", "ve", "act")

    dbg_tensors = {}

    def dbg(name, ap, rows, cols):
        if not debug:
            return
        d = nc.dram_tensor("dbg_" + name, [rows, cols], ap.dtype,
                           kind="ExternalOutput").ap()
        dbg_tensors[name] = d
        snap = [(s, s.n) for s in ALLS if s.n > 0]
        sch.dma("sp", d[:, :], ap, waits=snap, inc=s_st)

    def barrier():
        snap = [(s, s.n) for s in ALLS if s.n > 0]
        for e in ENGS:
            sch.op(e, "wait", waits=snap)

    def w(*sems):
        return [(s, s.n) for s in sems]

    sch.op("act", "table_name", setname="natural_log_exp_and_others")
    sch.dma("sp", id_bf.ap()[:, :], ident_bf_d.ap()[:, :], inc=s_sp)
    sch.dma("sp", id_f32.ap()[:, :], ident_f32_d.ap()[:, :], inc=s_sp)
    sch.dma("sp", lt_sb.ap()[:, :], lt_d.ap()[:, :], inc=s_sp)
    sch.dma("sp", onesm.ap()[:, :], onesm_d.ap()[:, :], inc=s_sp)
    sch.dma("sp", iota1.ap()[:, :], iota1_d.ap()[:, :], inc=s_sp)
    sch.dma("gp", bo_row.ap()[:, :], bo_d.rearrange("(a e) -> a e", a=1), inc=s_gp)
    sch.dma("gp", gb_row.ap()[:, :], gb_d.rearrange("(a e) -> a e", a=1), inc=s_gp)
    sch.dma("gp", b2_all.ap()[:, :], b2_d[:, :], inc=s_gp)
    sch.dma("sp", b1raw.ap()[:, :], b1_d.rearrange("a (x c) -> (a x) c", x=16), inc=s_sp)
    sch.dma("gp", bv33_r.ap()[:, :], bv_d.rearrange("(a e) -> a e", a=1), inc=s_gp)
    sch.op("ve", "ts", waits=w(s_gp), out=bv33_r.ap()[:, :], in_=None,
           in0=bv33_r.ap()[:, :], s1=float(NCROSS + 1), s2=None,
           op0=ALU.mult, op1=None, inc=s_ve)
    sch.op("gp", "memset", ap=ones_r.ap()[:, :], val=1.0, inc=s_ve)
    sch.op("gp", "memset", ap=ones_r_f32_guard.ap()[:, :], val=1.0, inc=s_ve)
    sch.dma("sp", gwraw.ap()[:, :], gw_d[:, :], inc=s_sp)

    ldv = w(s_sp)
    for j in range(2):
        sch.op("pe", "tp", waits=ldv if j == 0 else (),
               out=pb[1].ap()[:, j * 128:(j + 1) * 128],
               in_=b1raw.ap()[:, j * 128:(j + 1) * 128],
               ident=id_f32.ap()[:, :], inc=s_pe if j == 1 else None)
    sch.op("ve", "copy", waits=w(s_pe), out=b1_c.ap()[:, :],
           in_=pb[1].ap()[:, 0:256], inc=s_ve)

    for vec, dst, scale in ((bq_d, sbq_c, QSC), (bk_d, bk_c, 1.0),
                            (bv_d, bv33_c, float(NCROSS + 1))):
        sch.dma("sp", braw.ap()[:, :], vec.rearrange("(p k) -> p k", p=8),
                waits=w(s_pe), inc=s_sp)
        sch.op("pe", "tp", waits=w(s_sp, s_ve),
               out=pb[0].ap()[:, 0:8], in_=braw.ap()[:, :],
               ident=id_f32.ap()[0:8, 0:8], inc=s_pe)
        if scale == 1.0:
            sch.op("ve", "copy", waits=w(s_pe), out=dst.ap()[:, :],
                   in_=pb[0].ap()[:, 0:8], inc=s_ve)
        else:
            sch.op("ve", "ts", waits=w(s_pe), out=dst.ap()[:, :],
                   in0=pb[0].ap()[:, 0:8], s1=float(scale), s2=None,
                   op0=ALU.mult, op1=None, inc=s_ve)

    for c in range(NEIN):
        sch.op("pe", "tp", waits=w(s_ve) if c == 0 else (),
               out=pb[2].ap()[:, c * 8:(c + 1) * 8],
               in_=gwraw.ap()[:, c * 128:(c + 1) * 128],
               ident=id_f32.ap()[0:8, 0:8],
               inc=s_pe if c == NEIN - 1 else None)
    sch.op("ve", "copy", waits=w(s_pe), out=gwT.ap()[:, 0:NEIN * NEXP],
           in_=pb[2].ap()[:, 0:NEIN * NEXP], inc=s_ve)

    def b1col(k):
        return (k % 2) * 128 + (k // 2)

    ln_state = {"tok_hist": [], "bank_hist": {i: [] for i in range(8)}}

    def ln_group(xaps, x_waits, dsts, tp_banks, tokaps):
        g = len(xaps)
        st = stat.ap()
        for t, xap in enumerate(xaps):
            sch.op("ve", "bnst", waits=x_waits if t == 0 else (),
                   out=st[:, t * 12:t * 12 + 6], in_=xap[:, 0:512])
            sch.op("ve", "bnst", out=st[:, t * 12 + 6:t * 12 + 12],
                   in_=xap[:, 512:1024],
                   inc=s_ve if t == g - 1 else None)
        ve_stats = s_ve.n
        for t in range(g):
            sch.op("ve", "bnag", waits=[(s_ve, ve_stats)] if t == 0 else (),
                   out=st[:, 48 + 2 * t:50 + 2 * t],
                   in_=st[:, t * 12:t * 12 + 12],
                   inc=s_ve if t == g - 1 else None)
        ve_var = s_ve.n
        for t in range(g):
            sch.op("ve", "ts", waits=[(s_ve, ve_var)] if t == 0 else (),
                   out=st[:, t * 12:t * 12 + 1], in_=None,
                   in0=st[:, 49 + 2 * t:50 + 2 * t], s1=1e-5, s2=None,
                   op0=ALU.add, op1=None)
        for t in range(g):
            sch.op("ve", "ts", out=st[:, 56 + t:57 + t], in_=None,
                   in0=st[:, 49 + 2 * t:50 + 2 * t], s1=-0.5, s2=1.5,
                   op0=ALU.mult, op1=ALU.add,
                   inc=s_ve if t == g - 1 else None)
        for it in range(4):
            vy = s_ve.n
            for t in range(g):
                sch.op("ve", "tt", waits=[(s_ve, vy)] if t == 0 else (),
                       out=st[:, t * 12 + 1:t * 12 + 2],
                       in0=st[:, 56 + t:57 + t], in1=st[:, 56 + t:57 + t],
                       top=ALU.mult, inc=s_ve if t == g - 1 else None)
            vy = s_ve.n
            for t in range(g):
                sch.op("ve", "tt", waits=[(s_ve, vy)] if t == 0 else (),
                       out=st[:, t * 12 + 2:t * 12 + 3],
                       in0=st[:, t * 12:t * 12 + 1],
                       in1=st[:, t * 12 + 1:t * 12 + 2],
                       top=ALU.mult, inc=s_ve if t == g - 1 else None)
            vy = s_ve.n
            for t in range(g):
                sch.op("ve", "ts", waits=[(s_ve, vy)] if t == 0 else (),
                       out=st[:, t * 12 + 3:t * 12 + 4], in_=None,
                       in0=st[:, t * 12 + 2:t * 12 + 3], s1=-0.5, s2=1.5,
                       op0=ALU.mult, op1=ALU.add,
                       inc=s_ve if t == g - 1 else None)
            vy = s_ve.n
            for t in range(g):
                sch.op("ve", "tt", waits=[(s_ve, vy)] if t == 0 else (),
                       out=st[:, 56 + t:57 + t],
                       in0=st[:, 56 + t:57 + t], in1=st[:, t * 12 + 3:t * 12 + 4],
                       top=ALU.mult, inc=s_ve if t == g - 1 else None)
        ve_rstd = s_ve.n
        for t in range(g):
            sch.op("ve", "stt", waits=[(s_ve, ve_rstd)] if t == 0 else (),
                   out=st[:, 60 + t:61 + t], in0=st[:, 48 + 2 * t:49 + 2 * t],
                   scalar=-1.0, in1=st[:, 56 + t:57 + t],
                   op0=ALU.mult, op1=ALU.mult,
                   inc=s_ve if t == g - 1 else None)
        ve_lnb = s_ve.n
        ve_x_last = None
        for t, xap in enumerate(xaps):
            xw = [(s_ve, ve_lnb)] if t == 0 else []
            th = ln_state["tok_hist"]
            if len(th) >= 2:
                xw.append((s_pe, th[-2]))
            sch.op("ve", "ts", waits=xw, out=tokaps[t], in_=None,
                   in0=xap, s1=st[:, 56 + t:57 + t], s2=st[:, 60 + t:61 + t],
                   op0=ALU.mult, op1=ALU.add, inc=s_ve)
            ve_x_last = s_ve.n
            bank = tp_banks[t % len(tp_banks)]
            pw = [(s_ve, s_ve.n)]
            bh = ln_state["bank_hist"][bank]
            if bh:
                pw.append((s_ve, bh[-1]))
            for j in range(NEIN):
                sch.op("pe", "tp", waits=pw if j == 0 else (),
                       out=pbf(bank)[:, j * 128:(j + 1) * 128],
                       in_=tokaps[t][:, j * 128:(j + 1) * 128],
                       ident=id_bf.ap()[:, :],
                       inc=s_pe if j == NEIN - 1 else None)
            ln_state["tok_hist"].append(s_pe.n)
            sch.op("ve", "copy", waits=w(s_pe), out=dsts[t],
                   in_=pbf(bank).rearrange("p (a b) -> p a b", a=NEIN)[:, :, 0:128],
                   inc=s_ve)
            ln_state["bank_hist"][bank].append(s_ve.n)
        return ve_stats, ve_x_last

    for p in range(NPASS):
        barrier()
        prev_read = None
        for grp in range(NT // 4):
            r0 = p * T + grp * 512
            lw = []
            if prev_read is not None:
                lw = [(s_ve, max(prev_read[0], prev_read[1]))]
            sch.dma("sp", x_stage.ap()[:, :, :],
                    xk[r0:r0 + 512, :].rearrange("(a k) e -> k a e", k=128),
                    waits=lw, inc=s_sp)
            xw = w(s_sp)
            xaps = [x_stage.ap()[:, t, :] for t in range(4)]
            dsts = [xn_fm.ap()[:, :, grp * 512 + t * 128: grp * 512 + (t + 1) * 128]
                    for t in range(4)]
            tokaps = [xn_tok.ap()[:, grp * 4 + t, :] for t in range(4)]
            prev_read = ln_group(xaps, xw, dsts, (5, 7), tokaps)
        L_x_last = max(prev_read[0], prev_read[1])
        L_ve_end = s_ve.n

        if p == 0:
            dbg("xn_fm", xn_fm.ap()[:, :, :].rearrange("p a b -> p (a b)"), 128, NEIN * T)
        ve_gcopy = {}
        for t in range(NT):
            bank = t % 2
            pw = []
            if t == 0:
                pw.append((s_ve, L_ve_end))
            if t >= 2:
                pw.append((s_ve, ve_gcopy[t - 2]))
            for ein in range(NEIN):
                sch.op("pe", "mm", waits=pw if ein == 0 else (),
                       out=pb[bank].ap()[:, 0:8],
                       lhsT=xn_fm.ap()[:, ein, t * 128:(t + 1) * 128],
                       rhs=gwT.ap()[:, ein * 8:(ein + 1) * 8],
                       start=(ein == 0), stop=False)
            sch.op("pe", "mm", out=pb[bank].ap()[:, 0:8],
                   lhsT=ones_r.ap()[:, 0:128], rhs=gb_row.ap()[:, :],
                   start=False, stop=True, inc=s_pe)
            sch.op("ve", "copy", waits=w(s_pe), out=glog.ap()[:, t, :],
                   in_=pb[bank].ap()[:, 0:8], inc=s_ve)
            ve_gcopy[t] = s_ve.n
        for f in range(8):
            sch.op("act", "act", waits=w(s_ve) if f == 0 else (),
                   out=gtmp.ap()[:, f:f + 1], in_=ones_r_f32_guard.ap()[:, 0:1],
                   func=AF.Identity, bias=None, scale=None, accum=None,
                   inc=s_ac)
        sch.op("act", "act",
               out=gexp.ap()[:, :, :].rearrange("p a b -> p (a b)"),
               in_=glog.ap()[:, :, :].rearrange("p a b -> p (a b)"),
               func=AF.Exp, bias=np.float32(-44.0), scale=None, accum=None,
               inc=s_ac)
        ac_gexp = s_ac.n
        sch.op("ve", "red", waits=[(s_ac, ac_gexp)],
               out=gsum.ap()[:, :], in_=gexp.ap()[:, :, :],
               rop=ALU.add, negate=False, inc=s_ve)
        sch.op("ve", "recip", waits=w(s_ve), out=grec.ap()[:, :],
               in_=gsum.ap()[:, :], inc=s_ve)
        for t in range(NT):
            sch.op("ve", "red", waits=w(s_ve) if t == 0 else (),
                   out=gstat.ap()[:, 3:4], in_=gexp.ap()[:, t, :],
                   rop=ALU.max, negate=False, inc=s_ve)
            sch.op("ve", "ts", waits=w(s_ve), out=gmask.ap()[:, :], in_=None,
                   in0=gexp.ap()[:, t, :], s1=gstat.ap()[:, 3:4], s2=None,
                   op0=ALU.is_ge, op1=None, inc=s_ve)
            sch.op("ve", "stt", waits=w(s_ve), out=gmask.ap()[:, :],
                   in0=gmask.ap()[:, :], scalar=-1e30, in1=gexp.ap()[:, t, :],
                   op0=ALU.mult, op1=ALU.add, inc=s_ve)
            sch.op("ve", "red", waits=w(s_ve), out=gstat.ap()[:, 4:5],
                   in_=gmask.ap()[:, :], rop=ALU.max, negate=False, inc=s_ve)
            sch.op("ve", "ts", waits=w(s_ve), out=gmask.ap()[:, :], in_=None,
                   in0=gexp.ap()[:, t, :], s1=gstat.ap()[:, 4:5], s2=None,
                   op0=ALU.is_ge, op1=None, inc=s_ve)
            sch.op("ve", "stt", waits=w(s_ve), out=gate.ap()[:, t, :],
                   in0=gexp.ap()[:, t, :],
                   scalar=grec.ap()[:, t:t + 1], in1=gmask.ap()[:, :],
                   op0=ALU.mult, op1=ALU.mult, inc=s_ve)
        ve_gate = s_ve.n

        sch.op("ve", "ts", waits=[(s_ve, ve_gate)],
               out=mask_bf.ap()[:, :, :].rearrange("p a b -> p (a b)"), in_=None,
               in0=gate.ap()[:, :, :].rearrange("p a b -> p (a b)"),
               s1=0.0, s2=None, op0=ALU.is_gt, op1=None, inc=s_ve)
        sch.op("ve", "ts",
               out=mask_f.ap()[:, :, :].rearrange("p a b -> p (a b)"), in_=None,
               in0=gate.ap()[:, :, :].rearrange("p a b -> p (a b)"),
               s1=0.0, s2=None, op0=ALU.is_gt, op1=None, inc=s_ve)
        ve_masks = s_ve.n
        for t4 in range(NT // 4):
            for t in range(4):
                tw = ()
                if t == 0:
                    tw = [(s_ve, ve_gate if t4 == 0 else s_ve.n)]
                sch.op("pe", "tp", waits=tw,
                       out=pb[7].ap()[0:8, t * 128:(t + 1) * 128],
                       in_=gate.ap()[:, t4 * 4 + t, :],
                       ident=id_f32.ap()[:, :],
                       inc=s_pe if t == 3 else None)
            sch.op("ve", "copy", waits=w(s_pe),
                   out=gateT.ap()[:, t4 * 4:(t4 + 1) * 4, :]
                   .rearrange("p a b -> p (a b)"),
                   in_=pb[7].ap()[0:8, :], inc=s_ve)
        ve_gateT = s_ve.n
        for g in range(NGRP):
            for t in range(4):
                base = (g * 4 + t) * 8
                for tp_ in range(t + 1):
                    pw = []
                    if g == 0 and t == 0 and tp_ == 0:
                        pw.append((s_ve, ve_masks))
                    sch.op("pe", "mm", waits=pw,
                           out=pb[0].ap()[:, base:base + 8],
                           lhsT=(lt_sb.ap()[:, :] if tp_ == t else onesm.ap()[:, :]),
                           rhs=mask_bf.ap()[:, g * 4 + tp_, :],
                           start=(tp_ == 0), stop=(tp_ == t),
                           inc=s_pe if (tp_ == t) else None)
        sch.op("ve", "copy", waits=w(s_pe), out=cum_sb.ap()[:, :],
               in_=pb[0].ap()[:, 0:NGRP * 4 * NEXP], inc=s_ve)
        ve_cum = s_ve.n

        if p == 0:
            dbg("gate", gate.ap()[:, :, :].rearrange("p a b -> p (a b)"), 128, NT * NEXP)
            dbg("cum", cum_sb.ap()[:, :], 128, NGRP * 4 * NEXP)
        w1_done = {}
        w2_done = {}
        h_done_pe = {}
        y_done_pe = {}

        def emit_w1(i, waits=()):
            e, hg = divmod(i, NHG)
            sch.dma("gp", w1_sb[i % 2].ap()[:, :, :],
                    w1_d[e, :, hg * 1024:(hg + 1) * 1024]
                    .rearrange("(c k) h -> k c h", k=128),
                    waits=waits, inc=s_gw)
            w1_done[i] = s_gw.n

        def emit_w2(j, waits=()):
            i, half = divmod(j, 2)
            e, hg = divmod(i, NHG)
            sch.dma("gp", w2_sb[j % 2].ap()[:, :, :],
                    w2_d[e, hg * 1024:(hg + 1) * 1024,
                         half * 512:(half + 1) * 512]
                    .rearrange("(c k) o -> k c o", k=128),
                    waits=waits, inc=s_gw)
            w2_done[j] = s_gw.n

        emit_w1(0, waits=[(s_ve, L_x_last)])
        emit_w1(1)
        emit_w2(0)
        emit_w2(1)

        selA_done = {}
        selB_done = {}
        selB_pe = {}
        gather_done = {}
        gather_pe = {}
        ybf_done = {}
        scatter_ve = {}
        scatter_pe = {}
        hlast_ac = {}
        ylast_ve = {}
        h_hist = []
        b56_hist = []

        def build_sel(e):
            pw = [(s_ve, ve_cum)]
            if e >= 1:
                pw.append((s_pe, gather_pe[e - 1]))
                pw.append((s_pe, selB_pe[e - 1]))
            for ti in range(NT):
                col = ti * 8 + e
                sch.op("ve", "ts", waits=pw if ti == 0 else (),
                       out=SelA.ap()[:, ti, :], in_=None,
                       in0=iota1.ap()[:, :],
                       s1=cum_sb.ap()[:, col:col + 1],
                       s2=mask_f.ap()[:, ti, e:e + 1],
                       op0=ALU.is_equal, op1=ALU.mult)
                sch.op("ve", "ts",
                       out=SelAs.ap()[:, ti, :], in_=None,
                       in0=iota1.ap()[:, :],
                       s1=cum_sb.ap()[:, col:col + 1],
                       s2=gate.ap()[:, ti, e:e + 1],
                       op0=ALU.is_equal, op1=ALU.mult,
                       inc=s_ve if ti == NT - 1 else None)
            selA_done[e] = s_ve.n

        def build_selB(e):
            evacs = []
            for g in range(NGRP):
                for ci, (c0, cw) in enumerate(CC):
                    idx = g * 3 + ci
                    bank = 1 + idx % 2
                    pw = []
                    if idx == 0:
                        pw.append((s_ve, selA_done[e]))
                        if e >= 1:
                            pw.append((s_ve, scatter_ve[e - 1]))
                            pw.append((s_ac, gather_done[e - 1]))
                    if idx >= 2:
                        pw.append((s_ve, evacs[idx - 2]))
                    for ti in range(4):
                        sch.op("pe", "tp", waits=pw if ti == 0 else (),
                               out=pbf(bank)[0:cw, ti * 128:(ti + 1) * 128],
                               in_=SelAs.ap()[:, g * 4 + ti, c0:c0 + cw],
                               ident=id_bf.ap()[:, :],
                               inc=s_pe if ti == 3 else None)
                    sch.op("ve", "copy", waits=w(s_pe),
                           out=SelBs.ap()[0:cw, g * 3 + ci, :],
                           in_=pbf(bank)[0:cw, 0:512], inc=s_ve)
                    evacs.append(s_ve.n)
            selB_done[e] = s_ve.n
            selB_pe[e] = s_pe.n

        def gather(e):
            ngt = 0
            for fc in range(NEIN):
                for g in range(NGRP):
                    bank = 1 + ngt % 2
                    pw = []
                    if ngt == 0:
                        pw.append((s_ve, selA_done[e]))
                        pw.append((s_ve, selB_done[e]))
                        if e >= 1:
                            pw.append((s_pe, h_done_pe[(e - 1) * NHG + NHG - 1]))
                    elif ngt >= 2:
                        pw.append((s_ac, gather_done.get((e, ngt - 2), 0)))
                    for ti in range(4):
                        sch.op("pe", "mm", waits=pw if ti == 0 else (),
                               out=pb[bank].ap()[:, 0:CAP],
                               lhsT=xn_tok.ap()[:, g * 4 + ti, fc * 128:(fc + 1) * 128],
                               rhs=SelA.ap()[:, g * 4 + ti, :],
                               start=(ti == 0), stop=(ti == 3),
                               inc=s_pe if ti == 3 else None)
                    sch.op("act", "act", waits=w(s_pe),
                           out=Xg.ap()[:, fc, g, :], in_=pb[bank].ap()[:, 0:CAP],
                           func=AF.Identity, bias=None, scale=None, accum=None,
                           inc=s_ac)
                    gather_done[(e, ngt)] = s_ac.n
                    ngt += 1
            gather_pe[e] = s_pe.n
            gather_done[e] = s_ac.n

        for e in range(NEXP):
            build_sel(e)
            build_selB(e)
            gather(e)
            for hg in range(NHG):
                i = e * NHG + hg
                for hc in range(HCG):
                    for g in range(NGRP):
                        nh = len(h_hist)
                        bank = 3 + nh % 2
                        pw = []
                        if hc == 0 and g == 0:
                            pw.append((s_gw, w1_done[i]))
                            pw.append((s_ac, gather_done[e]))
                        if nh >= 2:
                            pw.append((s_ac, h_hist[nh - 2]))
                        for kc in range(NEIN):
                            sch.op("pe", "mm", waits=pw if kc == 0 else (),
                                   out=pb[bank].ap()[:, 0:CAP],
                                   lhsT=w1_sb[i % 2].ap()[:, kc, hc * 128:(hc + 1) * 128],
                                   rhs=Xg.ap()[:, kc, g, :],
                                   start=(kc == 0), stop=(kc == NEIN - 1),
                                   inc=s_pe if kc == NEIN - 1 else None)
                        sch.op("act", "act", waits=w(s_pe),
                               out=h_sb.ap()[:, hc, g, :],
                               in_=pb[bank].ap()[:, 0:CAP], func=AF.Relu,
                               bias=b1_c.ap()[:, b1col(e * 32 + hg * 8 + hc):
                                              b1col(e * 32 + hg * 8 + hc) + 1],
                               scale=None, accum=None, inc=s_ac)
                        h_hist.append(s_ac.n)
                h_done_pe[i] = s_pe.n
                hlast_ac[i] = s_ac.n
                if i + 2 < NEXP * NHG:
                    emit_w1(i + 2, waits=[(s_pe, h_done_pe[i])])
                for half in range(2):
                    j = i * 2 + half
                    for g in range(NGRP):
                        for ci, (c0, cw) in enumerate(CC):
                            nb = len(b56_hist)
                            bank = 5 + nb % 2
                            pw = []
                            if g == 0 and ci == 0:
                                if half == 0:
                                    pw.append((s_ac, hlast_ac[i]))
                                pw.append((s_gw, w2_done[j]))
                            if nb >= 2:
                                pw.append((s_ve, b56_hist[nb - 2]))
                            for kc in range(HCG):
                                sch.op("pe", "mm", waits=pw if kc == 0 else (),
                                       out=pb[bank].ap()[0:cw, :],
                                       lhsT=h_sb.ap()[:, kc, g, c0:c0 + cw],
                                       rhs=w2_sb[j % 2].ap()[:, kc, :],
                                       start=(kc == 0), stop=(kc == HCG - 1),
                                       inc=s_pe if kc == HCG - 1 else None)
                            dst = y_acc.ap()[0:cw, g * 3 + ci,
                                             half * 512:(half + 1) * 512]
                            if hg == 0:
                                sch.op("ve", "copy", waits=w(s_pe),
                                       out=dst, in_=pb[bank].ap()[0:cw, :],
                                       inc=s_ve)
                            else:
                                sch.op("ve", "stt", waits=w(s_pe),
                                       out=dst, in0=pb[bank].ap()[0:cw, :],
                                       scalar=1.0, in1=dst,
                                       op0=ALU.mult, op1=ALU.add, inc=s_ve)
                            b56_hist.append(s_ve.n)
                    y_done_pe[j] = s_pe.n
                    if j + 2 < NEXP * NHG * 2:
                        emit_w2(j + 2, waits=[(s_pe, y_done_pe[j])])
                ylast_ve[i] = s_ve.n
            for g in range(NGRP):
                for ci, (c0, cw) in enumerate(CC):
                    pw = [(s_ve, ylast_ve[e * NHG + NHG - 1])] if (g == 0 and ci == 0) else []
                    if e >= 1 and g == 0 and ci == 0:
                        pw.append((s_pe, scatter_pe[e - 1]))
                    sch.op("ve", "copy", waits=pw,
                           out=y_bf.ap()[0:cw, g * 3 + ci, :],
                           in_=y_acc.ap()[0:cw, g * 3 + ci, :],
                           inc=s_ve if (g == NGRP - 1 and ci == 2) else None)
            ybf_done[e] = s_ve.n
            nsct = 0
            for ti in range(NT):
                g = ti // 4
                for half in range(2):
                    nb = len(b56_hist)
                    bank = 5 + nb % 2
                    pw = []
                    if nsct == 0:
                        pw.append((s_ve, ybf_done[e]))
                        pw.append((s_ve, selB_done[e]))
                    if nb >= 2:
                        pw.append((s_ve, b56_hist[nb - 2]))
                    if e == 0:
                        sch.op("pe", "mm", waits=pw,
                               out=pb[bank].ap()[:, :],
                               lhsT=gateT.ap()[:, ti, :],
                               rhs=b2_all.ap()[:, half * 512:(half + 1) * 512],
                               start=True, stop=False)
                        pw = ()
                    for ci, (c0, cw) in enumerate(CC):
                        sch.op("pe", "mm", waits=pw if ci == 0 else (),
                               out=pb[bank].ap()[:, :],
                               lhsT=SelBs.ap()[0:cw, g * 3 + ci,
                                               (ti % 4) * 128:(ti % 4 + 1) * 128],
                               rhs=y_bf.ap()[0:cw, g * 3 + ci,
                                             half * 512:(half + 1) * 512],
                               start=(e > 0 and ci == 0),
                               stop=(ci == 2),
                               inc=s_pe if ci == 2 else None)
                        pw = ()
                    dst = out_acc.ap()[:, ti, half * 512:(half + 1) * 512]
                    if e == 0:
                        sch.op("ve", "copy", waits=w(s_pe),
                               out=dst, in_=pb[bank].ap()[:, :], inc=s_ve)
                    else:
                        sch.op("ve", "stt", waits=w(s_pe),
                               out=dst, in0=pb[bank].ap()[:, :],
                               scalar=1.0, in1=dst,
                               op0=ALU.mult, op1=ALU.add, inc=s_ve)
                    b56_hist.append(s_ve.n)
                    nsct += 1
            scatter_ve[e] = s_ve.n
            scatter_pe[e] = s_pe.n

        if p == 0:
            dbg("out_acc", out_acc.ap()[:, :, :].rearrange("p a b -> p (a b)"), 128, NT * E)
        barrier()
        sch.op("act", "table_name", setname="silu_and_others")
        for t in range(NT):
            sw = [(s_ve, scatter_ve[NEXP - 1])] if t == 0 else ()
            sch.op("act", "act", waits=sw,
                   out=g_sb.ap()[:, t, :], in_=out_acc.ap()[:, t, :],
                   func=AF.Silu, bias=None, scale=None, accum=None,
                   inc=s_ac if t == NT - 1 else None)
        sch.op("act", "table_name", setname="natural_log_exp_and_others")

        if p == 0:
            dbg("g", g_sb.ap()[:, :, :].rearrange("p a b -> p (a b)"), 128, NT * E)
        barrier()
        sch.dma("gp", Wa_sb.ap()[:, :, :],
                Wq_d.rearrange("(c k) o -> k c o", k=128),
                waits=[(s_ve, ybf_done[NEXP - 1])], inc=s_gw)
        wq_done = s_gw.n
        acq_t4 = {}
        nq = 0
        for t4 in range(NT4):
            for eo in range(NEIN):
                bank = nq % 2
                pw = []
                if nq == 0:
                    pw.append((s_gw, wq_done))
                    pw.append((s_ac, gather_done[NEXP - 1]))
                if nq >= 2:
                    pw.append((s_ac, acq_t4.get(("h", nq - 2), 0)))
                for ein in range(NEIN):
                    sch.op("pe", "mm", waits=pw if ein == 0 else (),
                           out=pb[bank].ap()[:, :],
                           lhsT=Wa_sb.ap()[:, ein, eo * 128:(eo + 1) * 128],
                           rhs=xn_fm.ap()[:, ein, t4 * 512:(t4 + 1) * 512],
                           start=(ein == 0), stop=(ein == NEIN - 1),
                           inc=s_pe if ein == NEIN - 1 else None)
                ew = [(s_pe, s_pe.n)]
                if nq == 0:
                    ew.append((s_ve, ybf_done[NEXP - 1]))
                    ew.append((s_pe, scatter_pe[NEXP - 1]))
                sch.op("act", "act", waits=ew,
                       out=Q_sb.ap()[:, eo, t4 * 512:(t4 + 1) * 512],
                       in_=pb[bank].ap()[:, :], func=AF.Identity,
                       bias=sbq_c.ap()[:, eo:eo + 1], scale=np.float32(QSC),
                       accum=None, inc=s_ac)
                acq_t4[("h", nq)] = s_ac.n
                nq += 1
            acq_t4[t4] = s_ac.n

        barrier()
        if p == 0:
            sch.dma("gp", Wa_sb.ap()[:, :, :],
                    Wk_d.rearrange("(c k) o -> k c o", k=128),
                    waits=w(s_pe), inc=s_gw)
            wk_done = s_gw.n
        prev_ln_read = None
        score_last = {}
        kevac = {}
        spill_done = {}
        stt_hist = []
        nsc = 0
        for mp in range(NMP):
            buf = mp % 2
            kq = ("sp", s_sp) if mp % 2 == 0 else ("gp", s_gw)
            if p == 0:
                if prev_ln_read is not None:
                    lw = [(s_ve, max(prev_ln_read[0], prev_ln_read[1]))]
                else:
                    lw = [(s_ve, L_x_last),
                          (s_pe, h_done_pe[NEXP * NHG - 2])]
                sch.dma("sp", x_stage.ap()[:, :, :],
                        xk[mp * 512:(mp + 1) * 512, :]
                        .rearrange("(a k) e -> k a e", k=128),
                        waits=lw, inc=s_sp)
                xw = w(s_sp)
                if mp == 0:
                    xw = xw + [(s_pe, gather_pe[NEXP - 1])]
                xaps = [x_stage.ap()[:, t, :] for t in range(4)]
                dsts = [xn_mp[buf].ap()[:, :, t * 128:(t + 1) * 128] for t in range(4)]
                tokaps = [xn_tok.ap()[:, t % 2, :] for t in range(4)]
                prev_ln_read = ln_group(xaps, xw, dsts, (7,), tokaps)
                xnmp_evac = s_ve.n
                for eo in range(NEIN):
                    bank = eo % 2
                    pw = []
                    if eo == 0:
                        pw.append((s_ve, xnmp_evac))
                        if mp == 0:
                            pw.append((s_gw, wk_done))
                    if eo >= 2:
                        pw.append((s_ac, kevac.get((buf, eo - 2), 0)))
                    for ein in range(NEIN):
                        sch.op("pe", "mm", waits=pw if ein == 0 else (),
                               out=pb[bank].ap()[:, :],
                               lhsT=Wa_sb.ap()[:, ein, eo * 128:(eo + 1) * 128],
                               rhs=xn_mp[buf].ap()[:, ein, :],
                               start=(ein == 0), stop=(ein == NEIN - 1),
                               inc=s_pe if ein == NEIN - 1 else None)
                    ew = [(s_pe, s_pe.n)]
                    if mp >= 2:
                        ew.append((s_pe, score_last.get(buf, 0)))
                        ew.append(spill_done[mp - 2])
                    sch.op("act", "act", waits=ew,
                           out=K_sb[buf].ap()[:, eo, :], in_=pb[bank].ap()[:, :],
                           func=AF.Identity, bias=bk_c.ap()[:, eo:eo + 1],
                           scale=None, accum=None, inc=s_ac)
                    kevac[(buf, eo)] = s_ac.n
                kevac_all = s_ac.n
                sch.dma(kq[0], kspill_d[mp], K_sb[buf].ap()[:, :, :],
                        waits=[(s_ac, kevac_all)], inc=kq[1])
                spill_done[mp] = (kq[1], kq[1].n)
                kw0 = [(s_ac, kevac_all)]
            else:
                lw = []
                if mp >= 2:
                    lw.append((s_pe, score_last.get(buf, 0)))
                else:
                    lw.append((s_pe, y_done_pe[NEXP * NHG * 2 - 1]))
                sch.dma(kq[0], K_sb[buf].ap()[:, :, :], kspill_d[mp],
                        waits=lw, inc=kq[1])
                spill_done[mp] = (kq[1], kq[1].n)
                kw0 = [spill_done[mp]]
            for n in range(NOWN):
                own_glob = p * NOWN + n
                for qt in range(2):
                    idx = nsc
                    bank = 2 + idx % 4
                    sbuf = idx % 2
                    pw = list(kw0) if (n == 0 and qt == 0) else []
                    if idx == 0:
                        pw.append((s_ac, gather_done[NEXP - 1]))
                        pw.append((s_ac, hlast_ac[NEXP * NHG - 1]))
                        pw.append((s_ve, scatter_ve[NEXP - 1]))
                    if qt == 0:
                        pw.append((s_ac, acq_t4[n // 2]))
                    if idx >= 4:
                        pw.append((s_ve, stt_hist[idx - 4]))
                    for ein in range(NEIN):
                        sch.op("pe", "mm", waits=pw if ein == 0 else (),
                               out=pb[bank].ap()[:, :],
                               lhsT=Q_sb.ap()[:, ein,
                                              n * 256 + qt * 128:n * 256 + (qt + 1) * 128],
                               rhs=K_sb[buf].ap()[:, ein, :],
                               start=(ein == 0), stop=(ein == NEIN - 1),
                               inc=s_pe if ein == NEIN - 1 else None)
                    pe_s = s_pe.n
                    rw = [(s_pe, pe_s)]
                    if idx >= 2:
                        rw.append((s_ve, stt_hist[idx - 2]))
                    for hh in range(2):
                        sch.op("act", "act", waits=rw if hh == 0 else (),
                               out=E_sb.ap()[:, sbuf, hh * 256:(hh + 1) * 256],
                               in_=pb[bank].ap()[:, hh * 256:(hh + 1) * 256],
                               func=AF.Exp, bias=np.float32(-44.0),
                               scale=None, accum=smx.ap()[:, sbuf, 2 + hh:3 + hh],
                               inc=s_ac if hh == 1 else None)
                    ac_e = s_ac.n
                    sch.op("ve", "recip", waits=[(s_ac, ac_e)],
                           out=smx.ap()[:, sbuf, 4:6], in_=smx.ap()[:, sbuf, 2:4],
                           inc=s_ve)
                    ve_rr = s_ve.n
                    dbl_half = None
                    if own_glob // 2 == mp:
                        dbl_half = own_glob % 2
                        sch.op("ve", "ts", waits=[(s_ve, ve_rr)], out=smx.ap()[:, sbuf, 6:7],
                               in_=None, in0=smx.ap()[:, sbuf, 4 + dbl_half:5 + dbl_half],
                               s1=2.0, s2=None, op0=ALU.mult, op1=None,
                               inc=s_ve)
                        ve_rr = s_ve.n
                    for hh in range(2):
                        sc = (smx.ap()[:, sbuf, 6:7] if hh == dbl_half
                              else smx.ap()[:, sbuf, 4 + hh:5 + hh])
                        if mp == 0 and hh == 0:
                            sch.op("ve", "ts", waits=[(s_ve, ve_rr)],
                                   out=P_acc.ap()[:, n, qt, :], in_=None,
                                   in0=E_sb.ap()[:, sbuf, hh * 256:(hh + 1) * 256],
                                   s1=sc, s2=None, op0=ALU.mult, op1=None,
                                   inc=s_ve)
                        else:
                            sch.op("ve", "stt",
                                   waits=[(s_ve, ve_rr if hh == 0 else s_ve.n)],
                                   out=P_acc.ap()[:, n, qt, :],
                                   in0=E_sb.ap()[:, sbuf, hh * 256:(hh + 1) * 256],
                                   scalar=sc, in1=P_acc.ap()[:, n, qt, :],
                                   op0=ALU.mult, op1=ALU.add,
                                   inc=s_ve)
                    stt_hist.append(s_ve.n)
                    nsc += 1
            score_last[buf] = s_pe.n

        if p == 0:
            dbg("P", P_acc.ap()[:, :, :, :].rearrange("p a b c -> p (a b c)"), 128, NOWN * 2 * CH)
        A4_dbg_pass = (p == 0)
        barrier()
        sch.dma("gp", Wa_sb.ap()[:, :, :],
                Wv_d.rearrange("(c k) o -> k c o", k=128), inc=s_gw)
        sch.dma("gp", Wb_sb.ap()[:, :, :],
                Wo_d.rearrange("(c k) o -> k c o", k=128), inc=s_gw)
        wvo_done = s_gw.n
        ac_v_hist = []
        pt_evac_last = 0
        ve_cmb_hist = []
        st_hist = []
        nv = 0
        nidx = 0
        for n in range(NOWN):
            for tt in range(2):
                for half in range(2):
                    bank = nv % 2
                    pw = []
                    if nv == 0:
                        pw.append((s_gw, wvo_done))
                    if nv >= 2:
                        pw.append((s_ac, ac_v_hist[nv - 2]))
                    for ein in range(NEIN):
                        sch.op("pe", "mm", waits=pw if ein == 0 else (),
                               out=pb[bank].ap()[:, :],
                               lhsT=xn_fm.ap()[:, ein,
                                               n * 256 + tt * 128:n * 256 + (tt + 1) * 128],
                               rhs=Wa_sb.ap()[:, ein, half * 512:(half + 1) * 512],
                               start=(ein == 0), stop=(ein == NEIN - 1),
                               inc=s_pe if ein == NEIN - 1 else None)
                    vw = [(s_pe, s_pe.n)]
                    sch.op("act", "act", waits=vw,
                           out=V_sb.ap()[:, tt, half * 512:(half + 1) * 512],
                           in_=pb[bank].ap()[:, :], func=AF.Identity, bias=None,
                           scale=None, accum=None, inc=s_ac)
                    ac_v_hist.append(s_ac.n)
                    nv += 1
            ac_v_all = s_ac.n
            ptw = []
            if pt_evac_last:
                ptw.append((s_ve, pt_evac_last))
            first = True
            for qt in range(2):
                for kt in range(2):
                    sch.op("pe", "tp", waits=ptw if first else (),
                           out=pb[2].ap()[:, kt * 256 + qt * 128:kt * 256 + (qt + 1) * 128],
                           in_=P_acc.ap()[:, n, qt, kt * 128:(kt + 1) * 128],
                           ident=id_f32.ap()[:, :],
                           inc=s_pe if (qt == 1 and kt == 1) else None)
                    first = False
            sch.op("ve", "copy", waits=w(s_pe), out=PT_sb.ap()[:, :, :],
                   in_=pb[2].ap().rearrange("p (a b) -> p a b", a=2), inc=s_ve)
            pt_evac_last = s_ve.n
            ve_comb = {}
            for ec in range(NEIN):
                bank = 3 + (ec // 2) % 2
                off = (ec % 2) * 256
                pw = []
                if ec == 0:
                    pw.append((s_ve, pt_evac_last))
                    pw.append((s_ac, ac_v_all))
                if ec % 2 == 0 and ec >= 4:
                    pw.append((s_ve, ve_comb[ec - 4]))
                for kt in range(2):
                    sch.op("pe", "mm", waits=pw if kt == 0 else (),
                           out=pb[bank].ap()[:, off:off + 256],
                           lhsT=V_sb.ap()[:, kt, ec * 128:(ec + 1) * 128],
                           rhs=PT_sb.ap()[:, kt, :],
                           start=(kt == 0), stop=False)
                sch.op("pe", "mm", out=pb[bank].ap()[:, off:off + 256],
                       lhsT=bv33_r.ap()[:, ec * 128:(ec + 1) * 128],
                       rhs=ones_r.ap()[:, 0:256],
                       start=False, stop=True,
                       inc=s_pe if ec % 2 == 1 else None)
                if ec % 2 == 1:
                    sch.op("ve", "copy", waits=w(s_pe),
                           out=comb.ap()[:, ec - 1:ec + 1, :]
                           .rearrange("p a b -> p (a b)"),
                           in_=pb[bank].ap()[:, :], inc=s_ve)
                    ve_comb[ec - 1] = s_ve.n
            comb_all_ve = s_ve.n
            for qt in range(2):
                obuf = nidx % 2
                xlw = []
                if len(ve_cmb_hist) >= 2:
                    xlw.append((s_ve, ve_cmb_hist[-2]))
                r0 = p * T + n * 256 + qt * 128
                sch.dma("sp", xrel.ap()[:, obuf, :], xk[r0:r0 + 128, :],
                        waits=xlw, inc=s_sp)
                sp_x = s_sp.n
                for half in range(2):
                    bank = 5 + (qt * 2 + half) % 2
                    pw = []
                    if qt == 0 and half == 0:
                        pw.append((s_ve, comb_all_ve))
                    if len(ve_cmb_hist) >= 2:
                        pw.append((s_ve, ve_cmb_hist[-2]))
                    for ec in range(NEIN):
                        sch.op("pe", "mm", waits=pw if ec == 0 else (),
                               out=pb[bank].ap()[:, :],
                               lhsT=comb.ap()[:, ec, qt * 128:(qt + 1) * 128],
                               rhs=Wb_sb.ap()[:, ec, half * 512:(half + 1) * 512],
                               start=(ec == 0), stop=False)
                    sch.op("pe", "mm", out=pb[bank].ap()[:, :],
                           lhsT=ones_r.ap()[:, 0:128],
                           rhs=bo_row.ap()[:, half * 512:(half + 1) * 512],
                           start=False, stop=True, inc=s_pe)
                    pe_sv = s_pe.n
                    t_glob = n * 2 + qt
                    cw = [(s_pe, pe_sv), (s_sp, sp_x)]
                    if len(st_hist) >= 2:
                        cw.append((s_st, st_hist[-2]))
                    sch.op("ve", "tt", waits=cw,
                           out=out_stg.ap()[:, obuf, half * 512:(half + 1) * 512],
                           in0=pb[bank].ap()[:, :],
                           in1=g_sb.ap()[:, t_glob, half * 512:(half + 1) * 512],
                           top=ALU.mult, inc=s_ve)
                    sch.op("ve", "tt", waits=w(s_ve),
                           out=out_stg.ap()[:, obuf, half * 512:(half + 1) * 512],
                           in0=out_stg.ap()[:, obuf, half * 512:(half + 1) * 512],
                           in1=xrel.ap()[:, obuf, half * 512:(half + 1) * 512],
                           top=ALU.add, inc=s_ve)
                    ve_cmb_hist.append(s_ve.n)
                ve_both = s_ve.n
                sch.dma("sp", out_d[r0:r0 + 128, :], out_stg.ap()[:, obuf, :],
                        waits=[(s_ve, ve_both)], inc=s_st)
                st_hist.append(s_st.n)
                nidx += 1
        if A4_dbg_pass:
            dbg("Vd", V_sb.ap()[:, :, :].rearrange("p a b -> p (a b)"), 128, 2 * E)

    sch.op("sp", "wait", waits=[(s_st, s_st.n)])

    for r in sch.recs:
        for sem, v in r.waits:
            assert v <= sem.n, f"unreachable wait {sem.name} {v} > {sem.n}"

    _emit(nc, sch)
    return nc


def _emit(nc, sch):
    from contextlib import ExitStack
    with ExitStack() as stack:
        for s in sch.sems:
            s.h = stack.enter_context(nc.semaphore(s.name))
        blk = stack.enter_context(nc.Block())
        last_wait = {}

        def emit_for(eng_name, eng_obj, is_queue=False):
            for r in sch.recs:
                if r.eng != eng_name:
                    continue
                for sem, v in r.waits:
                    key = (eng_name, sem.name)
                    if last_wait.get(key, 0) >= v:
                        continue
                    eng_obj.wait_ge(sem.h, v)
                    last_wait[key] = v
                a = r.args
                k = r.kind
                if k == "wait":
                    inst = None
                elif k == "dma":
                    inst = eng_obj.dma_start(out=a["out"], in_=a["in_"])
                elif k == "memset":
                    inst = eng_obj.memset(a["ap"], a["val"])
                elif k == "mm":
                    inst = eng_obj.matmul(a["out"], a["lhsT"], a["rhs"],
                                          start=a["start"], stop=a["stop"])
                elif k == "tp":
                    inst = eng_obj.transpose(a["out"], a["in_"], a["ident"])
                elif k == "act":
                    kw = {}
                    if a.get("bias") is not None:
                        kw["bias"] = a["bias"]
                    if a.get("scale") is not None:
                        kw["scale"] = a["scale"]
                    if a.get("accum") is not None:
                        kw["accum_out"] = a["accum"]
                    inst = eng_obj.activation(a["out"], a["in_"], a["func"], **kw)
                elif k == "table":
                    inst = eng_obj.preload_act_table(a["func"])
                elif k == "table_name":
                    from concourse.hw_specs import get_activation_tables
                    _tabs = list(get_activation_tables(nc.m.arch).keys())
                    inst = eng_obj.add_instruction(mybir.InstLoadActFuncSet(
                        name=nc.get_next_instruction_name(),
                        act_func_set_id=_tabs.index(a["setname"]),
                        ins=[], outs=[]))
                elif k == "red":
                    inst = eng_obj.tensor_reduce(a["out"], a["in_"], axis=AX.X,
                                                 op=a["rop"], negate=a["negate"])
                elif k == "tt":
                    inst = eng_obj.tensor_tensor(a["out"], a["in0"], a["in1"],
                                                 op=a["top"])
                elif k == "ts":
                    if a.get("op1") is None:
                        inst = eng_obj.tensor_scalar(a["out"], a["in0"], a["s1"],
                                                     None, a["op0"])
                    else:
                        inst = eng_obj.tensor_scalar(a["out"], a["in0"], a["s1"],
                                                     a["s2"], a["op0"], a["op1"])
                elif k == "stt":
                    inst = eng_obj.scalar_tensor_tensor(
                        a["out"], a["in0"], a["scalar"], a["in1"],
                        op0=a["op0"], op1=a["op1"])
                elif k == "bnst":
                    inst = eng_obj.bn_stats(a["out"], a["in_"])
                elif k == "bnag":
                    inst = eng_obj.bn_aggr(a["out"], a["in_"])
                elif k == "recip":
                    inst = eng_obj.reciprocal(a["out"], a["in_"])
                elif k == "copy":
                    inst = eng_obj.tensor_copy(a["out"], a["in_"])
                else:
                    raise ValueError(k)
                for sem, kk in r.incs:
                    assert inst is not None, f"inc on wait record {k}"
                    inst.then_inc(sem.h, kk)

        @blk.sync
        def _(sync):
            emit_for("sp", sync, True)

        @blk.gpsimd
        def _(gp):
            emit_for("gp", gp, True)

        @blk.tensor
        def _(te):
            emit_for("pe", te)

        @blk.vector
        def _(ve):
            emit_for("ve", ve)

        @blk.scalar
        def _(se):
            emit_for("act", se)



def kernel(L0_attn_x, L0_attn_Wq, L0_attn_bq, L0_attn_Wk, L0_attn_bk,
           L0_attn_Wv, L0_attn_bv, L0_attn_Wo, L0_attn_bo,
           L0_ffn_gate_w, L0_ffn_gate_b, L0_ffn_w1, L0_ffn_b1,
           L0_ffn_w2, L0_ffn_b2):
    x = np.ascontiguousarray(np.asarray(L0_attn_x, dtype=np.float32))
    weights = {
        "Wq": L0_attn_Wq, "bq": L0_attn_bq, "Wk": L0_attn_Wk, "bk": L0_attn_bk,
        "Wv": L0_attn_Wv, "bv": L0_attn_bv, "Wo": L0_attn_Wo, "bo": L0_attn_bo,
        "gw": L0_ffn_gate_w, "gb": L0_ffn_gate_b,
        "w1": L0_ffn_w1, "b1": L0_ffn_b1, "w2": L0_ffn_w2, "b2": L0_ffn_b2,
    }
    weights = {k: np.ascontiguousarray(np.asarray(v, dtype=np.float32))
               for k, v in weights.items()}

    nc = build_program()
    in_maps = []
    for c in range(8):
        b, h = divmod(c, 2)
        xb = x[b]
        xkc = np.ascontiguousarray(
            np.concatenate([xb[h * 4096:], xb[:h * 4096]], axis=0))
        m = {"xk": xkc}
        m.update(weights)
        in_maps.append(m)
    res = run_bass_kernel_spmd(nc, in_maps, core_ids=list(range(8)))
    out = np.empty((4, 8192, 1024), np.float32)
    for c in range(8):
        b, h = divmod(c, 2)
        out[b, h * 4096:(h + 1) * 4096] = res.results[c]["out"]
    return out
